# version 07
# baseline (speedup 1.0000x reference)
import math
import contextlib
import ml_dtypes
import numpy as np
import concourse.bass as bass
import concourse.mybir as mybir
from concourse.alu_op_type import AluOpType as OP
from concourse.bass_utils import run_bass_kernel_spmd
from concourse.library_overlay import lower_extended_insts
from concourse import library_config

F32 = mybir.dt.float32
BF16 = mybir.dt.bfloat16
I16 = mybir.dt.int16
I32 = mybir.dt.int32
U32 = mybir.dt.uint32
ACT = mybir.ActivationFunctionType
AXX = mybir.AxisListType.X

B, T, D = 4, 4096, 1024
TH = T // 2
NT = TH // 128
H0, DK = 16, 64
SEG = 64
I0 = 4096
E, DFF = 8, 4096
CAP = int(math.ceil(1.25 * (B * T) / E))
NS_E = 384
NSB = 896
NSLOT = 4 * NSB
CAP = 2560
L1W_SHAPES = {"dkv": (D, 256), "uv": (256, 512), "dq": (D, 512), "uq": (512, 512),
              "qr": (512, 256), "kr": (D, 32), "wout": (512, D), "iwq": (D, 256),
              "iwk": (D, 64), "gwT": (D, 8), "ukz": (128, 8 * 256), "idxw": (128, 4),
              "w1g": (4096, DFF), "w2g": (4 * DFF, D), "b1g": (128, 128), "b2g": (4, D)}
NEG_BIG = -3.38e38
SENT_CAUSAL = -1.0e30

ENGINES = ("tensor", "vector", "scalar", "gpsimd", "sync")
STAGE = "full"
N_PACKS = 8
N_STEPS = 64


class Arena:

    DTSIZE = {F32: 4, BF16: 2, I16: 2, I32: 4, U32: 4}

    def __init__(self, nc, stack, name, nbytes):
        self.t = stack.enter_context(nc.sbuf_tensor(name, [128, nbytes], mybir.dt.uint8))
        self.off = 0
        self.nbytes = nbytes

    def take(self, shape, dtype, align=64):
        assert shape[0] == 128, shape
        free = 1
        for s in shape[1:]:
            free *= s
        nb = free * self.DTSIZE[dtype]
        self.off = (self.off + align - 1) // align * align
        assert self.off + nb <= self.nbytes, f"arena overflow: {self.off}+{nb} > {self.nbytes}"
        ap = self.t[:, self.off:self.off + nb].bitcast(dtype)
        self.off += nb
        if len(shape) > 2:
            dims = " ".join(f"d{k}" for k in range(1, len(shape)))
            ap = ap.rearrange(f"p ({dims}) -> p {dims}",
                              **{f"d{k}": shape[k] for k in range(1, len(shape))})
        return ap

    def mark(self):
        return self.off

    def reset(self, mark):
        self.off = mark


class Tok:

    __slots__ = ("sem", "val")

    def __init__(self, sem, val):
        self.sem = sem
        self.val = val


class Prog:

    def __init__(self, nc, block, stack, nsems=40):
        self.nc = nc
        self.block = block
        self.pool = [stack.enter_context(nc.semaphore(f"sem{i}")) for i in range(nsems)]
        self.free = list(self.pool)
        self.selfsem = {e: self.free.pop() for e in ENGINES}
        self.selfcnt = {e: 0 for e in ENGINES}
        self.last = {e: None for e in ENGINES}
        self._dma = {}

    def new_sem(self):
        return self.free.pop()

    def _section(self, eng, fn):
        getattr(self.block, eng)(fn)

    def wait(self, eng, *toks):
        toks = [t for t in toks if t is not None]
        if not toks:
            return

        def fn(e):
            for t in toks:
                e.wait_ge(t.sem, t.val)

        self._section(eng, fn)

    def layer(self, eng, fn, waits=(), chain=True):
        wl = [t for t in waits if t is not None]
        if chain and self.last[eng] is not None:
            wl.append(self.last[eng])

        def body(e):
            for t in wl:
                e.wait_ge(t.sem, t.val)
            last = fn(e)
            if isinstance(last, (list, tuple)):
                last = last[-1]
            assert last is not None, f"layer on {eng} returned no instruction"
            last.then_inc(self.selfsem[eng], 1)

        self._section(eng, body)
        self.selfcnt[eng] += 1
        tok = Tok(self.selfsem[eng], self.selfcnt[eng])
        self.last[eng] = tok
        return tok

    def dma_tok(self, stream):
        if stream not in self._dma:
            return None
        s, c = self._dma[stream]
        return Tok(s, c)

    def phase_barrier(self):
        toks = [self.last[e] for e in ENGINES if self.last[e] is not None]
        toks += [Tok(s, c) for (s, c) in self._dma.values()]
        for e in ENGINES:
            self.wait(e, *toks)

    def dmas(self, eng, build, waits=(), chain=False, stream=None):
        key = stream or eng
        if key not in self._dma:
            self._dma[key] = [self.new_sem(), 0]
        ds = self._dma[key]
        wl = [t for t in waits if t is not None]
        if chain and self.last[eng] is not None:
            wl.append(self.last[eng])
        cnt = {"n": 0}

        def body(e):
            for t in wl:
                e.wait_ge(t.sem, t.val)

            def inc(instr):
                instr.then_inc(ds[0], 16)
                cnt["n"] += 1

            build(e, inc)

        self._section(eng, body)
        ds[1] += 16 * cnt["n"]
        return Tok(ds[0], ds[1])



def host_consts(core):
    b, j = core // 2, core % 2
    t0 = j * TH
    c = {}
    c["identity"] = np.eye(128, dtype=np.float32)
    m = np.zeros((SEG, SEG), np.float32)
    m[np.triu_indices(SEG, 1)] = -1.0e30
    c["causal0"] = np.concatenate([m, m], axis=0).astype(np.float32)
    c["ones"] = np.ones((128, 1), np.float32)
    c["jf"] = np.full((128, 1), float(j), np.float32)
    tp = t0 + np.arange(TH).reshape(NT, 128).T
    c["tpos"] = np.ascontiguousarray(tp.astype(np.float32))
    bm = np.zeros((128, 128), np.float32)
    bm[0:64, 0:64] = 1.0
    bm[64:128, 64:128] = 1.0
    c["bmask"] = bm
    hm = np.zeros((128, 2), np.float32)
    hm[0:64, 0] = 1.0
    hm[64:128, 1] = 1.0
    c["hmask"] = hm
    half = 16
    ar32 = np.arange(half, dtype=np.float32)
    theta = (1.0 / (10000.0 ** (2.0 * ar32 / 32.0))).astype(np.float64)
    th32 = np.concatenate([theta, theta])
    tq = np.arange(t0, t0 + TH, dtype=np.float64)
    angL = (2.0 * tq - 63.0)
    angS = tq
    for nm, ang, fn in (("cQLs", angL, np.cos), ("sQLs", angL, np.sin),
                        ("cQSs", angS, np.cos), ("sQSs", angS, np.sin)):
        t32 = fn(ang[None, :] * th32[:, None]).astype(np.float32)
        c[nm] = np.ascontiguousarray(np.tile(t32, (4, 1)))
    ku = np.maximum(np.arange(T, dtype=np.float64) + (j - 1) * 2048.0, 0.0)
    c["cosK"] = np.ascontiguousarray(np.cos(ku[None, :] * th32[:, None]).astype(np.float32))
    c["sinK"] = np.ascontiguousarray(np.sin(ku[None, :] * th32[:, None]).astype(np.float32))
    c["clampu"] = np.full((128, 1), 2048.0 * (1 - j), np.float32)
    c["upos"] = np.ascontiguousarray((2048.0 + np.arange(TH).reshape(NT, 128).T).astype(np.float32))
    cc_ = np.arange(128, dtype=np.float64)
    rg = 64.0 + (cc_ % 32.0)
    r0 = np.minimum(cc_, 95.0)
    c["cosRg"] = np.ascontiguousarray(np.cos(rg[None, :] * th32[:, None]).astype(np.float32))
    c["sinRg"] = np.ascontiguousarray(np.sin(rg[None, :] * th32[:, None]).astype(np.float32))
    c["cosR0"] = np.ascontiguousarray(np.cos(r0[None, :] * th32[:, None]).astype(np.float32))
    c["sinR0"] = np.ascontiguousarray(np.sin(r0[None, :] * th32[:, None]).astype(np.float32))
    rp = np.zeros((32, 128), np.float32)
    for a in range(4):
        rp[np.arange(32), a * 32 + np.arange(32)] = 1.0
    c["rep4"] = rp
    c["sgn16"] = np.ascontiguousarray(np.tile(np.concatenate(
        [-np.ones(16, np.float32), np.ones(16, np.float32)]), 4)[:, None])
    pp, ccj = np.meshgrid(np.arange(128), np.arange(128), indexing="ij")
    c["cm01"] = (ccj <= pp).astype(np.float32)
    c["cmneg"] = ((ccj > pp) * (-1.0e30)).astype(np.float32)
    pp2, cc2 = np.meshgrid(np.arange(128), np.arange(256), indexing="ij")
    inw = (cc2 - pp2 >= 65) & (cc2 - pp2 <= 128)
    c["b01"] = (~inw).astype(np.float32)
    c["bneg"] = (inw * (-2.0e30)).astype(np.float32)
    inw0 = inw.copy()
    if j == 0:
        for p_ in range(63):
            inw0[p_, :] = False
            inw0[p_, 128:192] = True
    c["b01t0"] = (~inw0).astype(np.float32)
    c["bnegt0"] = (inw0 * (-2.0e30)).astype(np.float32)
    bm01 = np.zeros((128, 8, 256), np.float32)
    bmng = np.zeros((128, 8, 256), np.float32)
    qrow = np.arange(128) // 8
    for g in range(8):
        tqg = 16 * g + qrow
        for cb in range(256):
            du = tqg - (cb - 128) if g < 4 else tqg - cb
            inb = (du >= 0) & (du <= 63)
            if g >= 4:
                inb = inb & (cb < 128)
            bm01[:, g, cb] = inb
            bmng[:, g, cb] = (~inb) * (-1.0e30)
    c["bm01g"] = np.ascontiguousarray(bm01.reshape(128, 2048))
    c["bmngg"] = np.ascontiguousarray(bmng.reshape(128, 2048))
    mck = np.zeros((128, 4, 128), np.float32)
    for ck in range(4):
        mck[:, ck, :] = ((np.arange(128)[None, :] // 8) == (np.arange(128)[:, None] // 32 + 4 * ck))
    c["mck01"] = np.ascontiguousarray(mck.reshape(128, 512))
    c["LTinc"] = np.tril(np.ones((128, 128), np.float32))
    c["riota"] = np.ascontiguousarray(np.broadcast_to(
        np.arange(128, dtype=np.float32)[None, :], (128, 128)).copy())
    c["iotaMB"] = np.ascontiguousarray(np.broadcast_to(
        np.arange(8, dtype=np.float32)[None, :], (128, 8)).copy() - 1.0e9)
    c["erow"] = (np.arange(8, dtype=np.float32)[:, None] // 4 * (4.0 * NSB)
                 + np.arange(8, dtype=np.float32)[:, None] % 4 * NSB).astype(np.float32)
    c["e896"] = (np.arange(8, dtype=np.float32)[:, None] * NSB).astype(np.float32)
    c["corelt"] = (np.arange(8, dtype=np.float32)[:, None] < core).astype(np.float32)
    c["piota"] = np.ascontiguousarray(np.arange(128, dtype=np.float32)[:, None])
    c["pre"] = np.ascontiguousarray(
        (np.arange(8, dtype=np.float32)[None, :] < core).astype(np.float32) * np.ones((128, 1), np.float32))
    return c


CONST_SHAPES = {k: v.shape for k, v in host_consts(0).items()}



def emit_ln(p, nc, stack, x_aps, xn_aps, tag, waits=(), dump=None, scr=None):
    n = len(x_aps)
    if scr is not None:
        stm, sts, str_, stt = (a[:, 0:n] for a in scr)
    else:
        stm = stack.enter_context(nc.sbuf_tensor(f"lnm_{tag}", [128, n], F32))
        sts = stack.enter_context(nc.sbuf_tensor(f"lnv_{tag}", [128, n], F32))
        str_ = stack.enter_context(nc.sbuf_tensor(f"lnr_{tag}", [128, n], F32))
        stt = stack.enter_context(nc.sbuf_tensor(f"lnt_{tag}", [128, n], F32))
    if dump is None:
        dump = stack.enter_context(nc.sbuf_tensor(f"lnd_{tag}", [128, D], F32))

    t_stats = None
    for i in range(n):
        p.layer("scalar", lambda s, i=i: s.activation(dump[:, :], x_aps[i], ACT.Identity,
                                                      accum_out=stm[:, i:i + 1]), waits=waits)
        t_stats = p.layer("scalar", lambda s, i=i: s.activation(dump[:, :], x_aps[i], ACT.Square,
                                                                accum_out=sts[:, i:i + 1]))

    p.layer("vector", lambda v: v.tensor_scalar(stm[:, :], stm[:, :], 1.0 / D, None, op0=OP.mult),
            waits=(t_stats,))
    p.layer("vector", lambda v: v.tensor_scalar(sts[:, :], sts[:, :], 1.0 / D, 1e-5, op0=OP.mult, op1=OP.add))
    p.layer("vector", lambda v: v.tensor_tensor(stt[:, :], stm[:, :], stm[:, :], op=OP.mult))
    t_v3 = p.layer("vector", lambda v: v.tensor_tensor(sts[:, :], sts[:, :], stt[:, :], op=OP.subtract))
    t_sq = p.layer("scalar", lambda s: s.activation(str_[:, :], sts[:, :], ACT.Sqrt), waits=(t_v3,))
    p.layer("vector", lambda v: v.reciprocal(str_[:, :], str_[:, :]), waits=(t_sq,))
    p.layer("vector", lambda v: v.tensor_tensor(stt[:, :], str_[:, :], str_[:, :], op=OP.mult))
    p.layer("vector", lambda v: v.tensor_tensor(stt[:, :], stt[:, :], sts[:, :], op=OP.mult))
    p.layer("vector", lambda v: v.tensor_scalar(stt[:, :], stt[:, :], -0.5, 1.5, op0=OP.mult, op1=OP.add))
    p.layer("vector", lambda v: v.tensor_tensor(str_[:, :], str_[:, :], stt[:, :], op=OP.mult))

    def f_norm(v):
        out = None
        for i in range(n):
            out = v.tensor_scalar(xn_aps[i], x_aps[i], stm[:, i:i + 1], str_[:, i:i + 1],
                                  op0=OP.subtract, op1=OP.mult)
        return out

    return p.layer("vector", f_norm)


class QScratch:

    def __init__(self, nc, stack, nbmax, tag):
        self.t = [stack.enter_context(nc.sbuf_tensor(f"qs{k}_{tag}", [128, nbmax], F32))
                  for k in range(8)]
        self.nbmax = nbmax

    def aps(self, nb):
        return [x[:, 0:nb] for x in self.t]


class NVScratch2:

    def __init__(self, A, cmax=1024):
        self.ax = A.take((128, cmax), F32)
        self.sg = A.take((128, cmax), F32)
        self.aa = A.take((128, cmax), F32)
        self.ab = A.take((128, cmax), F32)
        self.qs = self
        self.t8 = [A.take((128, cmax // 16), F32) for _ in range(8)]
        self.cmax = cmax

    def aps(self, nb):
        return [x[:, 0:nb] for x in self.t8]


def emit_fp8_rne(p, sc, sc8, qs, nb):
    g, th, c, t1, m, f, u1, u2 = qs.aps(nb)
    V = lambda fn: p.layer("vector", fn)
    V(lambda v: v.tensor_scalar(c, sc, 4.0, None, op0=OP.is_ge))
    V(lambda v: v.tensor_scalar(t1, c, 255.0, 1.0, op0=OP.mult, op1=OP.add))
    V(lambda v: v.tensor_scalar(g, t1, 2.0 ** -9, None, op0=OP.mult))
    V(lambda v: v.tensor_scalar(th, t1, 0.25, None, op0=OP.mult))
    V(lambda v: v.tensor_tensor(c, sc, th, op=OP.is_ge))
    V(lambda v: v.tensor_scalar(t1, c, 15.0, 1.0, op0=OP.mult, op1=OP.add))
    V(lambda v: v.tensor_tensor(g, g, t1, op=OP.mult))
    V(lambda v: v.tensor_scalar(t1, c, 3.75, 0.25, op0=OP.mult, op1=OP.add))
    V(lambda v: v.tensor_tensor(th, th, t1, op=OP.mult))
    V(lambda v: v.tensor_tensor(c, sc, th, op=OP.is_ge))
    V(lambda v: v.tensor_scalar(t1, c, 3.0, 1.0, op0=OP.mult, op1=OP.add))
    V(lambda v: v.tensor_tensor(g, g, t1, op=OP.mult))
    V(lambda v: v.tensor_scalar(t1, c, 1.5, 0.5, op0=OP.mult, op1=OP.add))
    V(lambda v: v.tensor_tensor(th, th, t1, op=OP.mult))
    V(lambda v: v.tensor_tensor(c, sc, th, op=OP.is_ge))
    V(lambda v: v.tensor_scalar(t1, c, 1.0, 1.0, op0=OP.mult, op1=OP.add))
    V(lambda v: v.tensor_tensor(g, g, t1, op=OP.mult))
    V(lambda v: v.reciprocal(t1, g))
    V(lambda v: v.tensor_tensor(m, sc, t1, op=OP.mult))
    V(lambda v: v.tensor_scalar(c, m, 8.0, None, op0=OP.is_ge))
    V(lambda v: v.tensor_scalar(f, c, 8.0, None, op0=OP.mult))
    V(lambda v: v.tensor_scalar(th, f, 4.0, None, op0=OP.add))
    V(lambda v: v.tensor_tensor(c, m, th, op=OP.is_ge))
    V(lambda v: v.scalar_tensor_tensor(f, c, 4.0, f, op0=OP.mult, op1=OP.add))
    V(lambda v: v.tensor_scalar(th, f, 2.0, None, op0=OP.add))
    V(lambda v: v.tensor_tensor(c, m, th, op=OP.is_ge))
    V(lambda v: v.scalar_tensor_tensor(f, c, 2.0, f, op0=OP.mult, op1=OP.add))
    V(lambda v: v.tensor_scalar(th, f, 1.0, None, op0=OP.add))
    V(lambda v: v.tensor_tensor(c, m, th, op=OP.is_ge))
    V(lambda v: v.scalar_tensor_tensor(f, c, 1.0, f, op0=OP.mult, op1=OP.add))
    V(lambda v: v.tensor_tensor(m, m, f, op=OP.subtract))
    V(lambda v: v.tensor_scalar(u1, m, 0.5, None, op0=OP.is_gt))
    V(lambda v: v.tensor_scalar(u2, m, 0.5, None, op0=OP.is_ge))
    V(lambda v: v.tensor_tensor(u2, u2, u1, op=OP.subtract))
    V(lambda v: v.tensor_tensor(u2, u2, c, op=OP.mult))
    V(lambda v: v.tensor_tensor(u1, u1, u2, op=OP.add))
    V(lambda v: v.tensor_tensor(f, f, u1, op=OP.add))
    return V(lambda v: v.tensor_tensor(sc8, f, g, op=OP.mult))


class NVScratch:

    def __init__(self, nc, stack, cmax, tag):
        self.ax = stack.enter_context(nc.sbuf_tensor(f"nv_ax_{tag}", [128, cmax], F32))
        self.sg = stack.enter_context(nc.sbuf_tensor(f"nv_sg_{tag}", [128, cmax], F32))
        self.aa = stack.enter_context(nc.sbuf_tensor(f"nv_aa_{tag}", [128, cmax], F32))
        self.ab = stack.enter_context(nc.sbuf_tensor(f"nv_ab_{tag}", [128, cmax], F32))
        self.qs = QScratch(nc, stack, cmax // 16, tag)
        self.cmax = cmax


def emit_nvfp4_quant(p, x_ap, out_ap, sc8_out, gs_ap, rgs_ap, inv6_ap, ns, C, waits=()):
    nb = C // 16
    assert C <= ns.cmax
    ax, sg, aa, ab = ns.ax[:, 0:C], ns.sg[:, 0:C], ns.aa[:, 0:C], ns.ab[:, 0:C]
    g, th, c, t1, m, f, u1, u2 = ns.qs.aps(nb)
    waits = tuple(waits) + (p.last["vector"],)
    t_ax = p.layer("scalar", lambda s: s.activation(ax, x_ap, ACT.Abs), waits=waits)
    t_sg = p.layer("scalar", lambda s: s.activation(sg, x_ap, ACT.Sign))
    p.layer("vector", lambda v: v.tensor_reduce(
        m, ax.rearrange("p (b k) -> p b k", k=16), axis=AXX, op=OP.max), waits=(t_ax,))
    p.layer("vector", lambda v: v.tensor_scalar(m, m, inv6_ap, gs_ap, op0=OP.mult, op1=OP.mult))
    p.layer("vector", lambda v: v.tensor_scalar(m, m, 448.0, None, op0=OP.min))
    emit_fp8_rne(p, m, sc8_out, ns.qs, nb)
    p.layer("vector", lambda v: v.tensor_scalar(t1, sc8_out, rgs_ap, None, op0=OP.mult))
    p.layer("vector", lambda v: v.tensor_scalar(c, t1, 0.0, None, op0=OP.is_equal))
    p.layer("vector", lambda v: v.tensor_tensor(t1, t1, c, op=OP.add))
    p.layer("vector", lambda v: v.reciprocal(t1, t1))
    ax3 = ax.rearrange("p (b k) -> p b k", k=16)
    aa3 = aa.rearrange("p (b k) -> p b k", k=16)
    ab3 = ab.rearrange("p (b k) -> p b k", k=16)
    t1b = t1.unsqueeze(2).broadcast_to((128, nb, 16))
    sc8b = sc8_out.unsqueeze(2).broadcast_to((128, nb, 16))
    p.layer("vector", lambda v: v.tensor_tensor(ax3, ax3, t1b, op=OP.mult))
    p.layer("vector", lambda v: v.tensor_scalar(aa, ax, 0.25, None, op0=OP.is_ge))
    p.layer("vector", lambda v: v.scalar_tensor_tensor(aa, ax, 0.75, aa, op0=OP.is_ge, op1=OP.add))
    p.layer("vector", lambda v: v.scalar_tensor_tensor(aa, ax, 1.25, aa, op0=OP.is_ge, op1=OP.add))
    p.layer("vector", lambda v: v.scalar_tensor_tensor(aa, ax, 1.75, aa, op0=OP.is_ge, op1=OP.add))
    p.layer("vector", lambda v: v.tensor_scalar(ab, ax, 2.5, None, op0=OP.is_ge))
    p.layer("vector", lambda v: v.scalar_tensor_tensor(ab, ax, 3.5, ab, op0=OP.is_ge, op1=OP.add))
    p.layer("vector", lambda v: v.scalar_tensor_tensor(ab, ax, 5.0, ab, op0=OP.is_ge, op1=OP.add))
    p.layer("vector", lambda v: v.scalar_tensor_tensor(ab, ax, 5.0, ab, op0=OP.is_ge, op1=OP.add))
    p.layer("vector", lambda v: v.scalar_tensor_tensor(aa, aa, 0.5, ab, op0=OP.mult, op1=OP.add))
    p.layer("vector", lambda v: v.tensor_tensor(aa3, aa3, sc8b, op=OP.mult))
    return p.layer("vector", lambda v: v.tensor_tensor(out_ap, aa, sg, op=OP.mult), waits=(t_sg,))


def emit_global_absmax(p, nc, stack, src_hbm, R, C, ident_ap, ones_row_ap, out_ap, tag,
                       ldbuf, ps):
    part = stack.enter_context(nc.sbuf_tensor(f"wmx_p_{tag}", [128, 1], F32))
    acc = stack.enter_context(nc.sbuf_tensor(f"wmx_a_{tag}", [128, 1], F32))
    row = stack.enter_context(nc.sbuf_tensor(f"wmx_r_{tag}", [1, 128], F32))
    one1 = stack.enter_context(nc.sbuf_tensor(f"wmx_o_{tag}", [1, 1], F32))
    p.layer("vector", lambda v: v.memset(acc[:, :], 0.0))
    nchunk = R // 128
    for ch in range(nchunk):
        t_ld = p.dmas("sync", lambda y, inc, ch=ch: inc(
            y.dma_start(out=ldbuf[:, :], in_=src_hbm[ch * 128:(ch + 1) * 128, :])),
            waits=(p.last["vector"],), stream="ld")
        p.layer("vector", lambda v: v.tensor_reduce(part[:, :], ldbuf[:, :], axis=AXX,
                                                    op=OP.max, apply_absolute_value=True),
                waits=(t_ld,))
        p.layer("vector", lambda v: v.tensor_tensor(acc[:, :], acc[:, :], part[:, :], op=OP.max))
    t_tr = p.layer("tensor", lambda t: t.transpose(ps[0:1, 0:128], acc[:, :], ident_ap),
                   waits=(p.last["vector"],))
    p.layer("vector", lambda v: v.tensor_copy(row[:, :], ps[0:1, 0:128]), waits=(t_tr,))
    t_rd = p.layer("vector", lambda v: v.tensor_reduce(one1[:, :], row[:, :], axis=AXX, op=OP.max))
    t_bc = p.layer("tensor", lambda t: t.matmul(ps[:, 0:1], ones_row_ap, one1[:, :],
                                                start=True, stop=True), waits=(t_rd,))
    return p.layer("vector", lambda v: v.tensor_copy(out_ap, ps[:, 0:1]), waits=(t_bc,))


def emit_wquant(p, nc, w_hbm, wq_hbm, R, C, gs_ap, rgs_ap, inv6_ap, ns, ldbuf, qbuf, scdump):
    CQ = min(C, ns.cmax)
    nchunk = R // 128
    t_st = p.dma_tok("sp")
    for ch in range(nchunk):
        t_ld = p.dmas("sync", lambda y, inc, ch=ch: inc(
            y.dma_start(out=ldbuf[:, :], in_=w_hbm[ch * 128:(ch + 1) * 128, :])),
            waits=(p.last["vector"], p.last["scalar"]), stream="ld")
        if t_st is not None:
            p.wait("vector", t_st)
        tq = None
        for c0 in range(0, C, CQ):
            tq = emit_nvfp4_quant(p, ldbuf[:, c0:c0 + CQ], qbuf[:, c0:c0 + CQ],
                                  scdump[:, c0 // 16:(c0 + CQ) // 16], gs_ap, rgs_ap,
                                  inv6_ap, ns, CQ, waits=(t_ld,))
        t_st = p.dmas("sync", lambda y, inc, ch=ch: inc(
            y.dma_start(out=wq_hbm[ch * 128:(ch + 1) * 128, :], in_=qbuf[:, :])),
            waits=(tq,), stream="sp")
    return t_st



def build_program(stage="l0", debug=False):
    nc = bass.Bass("TRN2", target_bir_lowering=False)
    stack = contextlib.ExitStack()
    xa = nc.dram_tensor("xa", [TH, D], F32, kind="ExternalInput").ap()
    xp = nc.dram_tensor("xp", [TH, D], F32, kind="ExternalInput").ap()
    out = nc.dram_tensor("out", [TH, D], F32, kind="ExternalOutput").ap()
    gw = nc.dram_tensor("gw", [I0, D], F32, kind="ExternalInput").ap()
    uw = nc.dram_tensor("uw", [I0, D], F32, kind="ExternalInput").ap()
    dwn = nc.dram_tensor("dwn", [D, I0], F32, kind="ExternalInput").ap()
    wqT = nc.dram_tensor("wqT", [D, D], F32, kind="ExternalInput").ap()
    wkT = nc.dram_tensor("wkT", [D, D], F32, kind="ExternalInput").ap()
    wvT = nc.dram_tensor("wvT", [D, D], F32, kind="ExternalInput").ap()
    woT = nc.dram_tensor("woT", [D, D], F32, kind="ExternalInput").ap()
    beta_in = nc.dram_tensor("beta_in", [128, H0], F32, kind="ExternalInput").ap()
    gsx_in = nc.dram_tensor("gsx_in", [128, 1], F32, kind="ExternalInput").ap()
    gsh_in = nc.dram_tensor("gsh_in", [128, 1], F32, kind="ExternalInput").ap()
    cin = {k: nc.dram_tensor(k, list(s), F32, kind="ExternalInput").ap()
           for k, s in CONST_SHAPES.items()}
    identb_in = nc.dram_tensor("identb_in", [128, 128], BF16, kind="ExternalInput").ap()
    gq = nc.dram_tensor("gq", [I0, D], BF16).ap()
    uq = nc.dram_tensor("uq", [I0, D], BF16).ap()
    dq2 = nc.dram_tensor("dq2", [D, I0], BF16).ap()
    g_hbm = nc.dram_tensor("g_hbm", [TH, D], F32).ap()
    xnfm_h = nc.dram_tensor("xnfm_h", [D, T], BF16).ap()
    xnfm_l = nc.dram_tensor("xnfm_l", [D, T], BF16).ap()
    as_h = nc.dram_tensor("as_h", [D, TH], BF16).ap()
    as_l = nc.dram_tensor("as_l", [D, TH], BF16).ap()

    ps = [stack.enter_context(nc.psum_tensor(f"ps{k}", [128, 512], F32)) for k in range(8)]
    A = Arena(nc, stack, "arena", 206 * 1024)

    ident = A.take((128, 128), F32)
    ones_row128 = A.take((128, 128), F32)
    tpos = A.take((128, NT), F32)
    beta = A.take((128, H0), F32)
    sc1 = {}
    for k in ("gsx", "rgsx", "gsh", "rgsh", "inv6", "six",
              "wm_g", "wm_u", "wm_d", "gs_g", "rgs_g", "gs_u", "rgs_u", "gs_d", "rgs_d",
              "s1", "s2", "sd", "jf"):
        sc1[k] = A.take((128, 1), F32)
    causal_sb = A.take((128, 64), F32)
    identb = A.take((128, 128), BF16)
    piota_sb = A.take((128, 1), F32)
    m_persist = A.mark()
    x1_t = A.take((128, NT, D), F32)
    xqT = A.take((128, 8, TH), BF16)

    m_base = A.mark()

    block = stack.enter_context(nc.Block())
    p = Prog(nc, block, stack)

    ns = NVScratch2(A)
    xq_tm = A.take((128, D), BF16)
    hl_h = A.take((128, D), BF16)
    hl_l = A.take((128, D), BF16)
    stage_h = A.take((128, 8, 128), BF16)
    stage_l = A.take((128, 8, 128), BF16)
    xtile = A.take((128, D), F32)
    lnscr = tuple(A.take((128, 4), F32) for _ in range(4))
    dumpD = A.take((128, D), F32)
    scdmp = A.take((128, 64), F32)

    def loads(y, inc):
        inc(y.dma_start(out=ident[:, :], in_=cin["identity"][:, :]))
        inc(y.dma_start(out=tpos[:, :], in_=cin["tpos"][:, :]))
        inc(y.dma_start(out=beta[:, :], in_=beta_in[:, :]))
        inc(y.dma_start(out=sc1["gsx"][:, :], in_=gsx_in[:, :]))
        inc(y.dma_start(out=sc1["gsh"][:, :], in_=gsh_in[:, :]))
        inc(y.dma_start(out=sc1["jf"][:, :], in_=cin["jf"][:, :]))
        inc(y.dma_start(out=causal_sb[:, :], in_=cin["causal0"][:, :]))
        inc(y.dma_start(out=identb[:, :], in_=identb_in[:, :]))
        inc(y.dma_start(out=piota_sb[:, :], in_=cin["piota"][:, :]))

    t_ld = p.dmas("sync", loads, stream="ld")
    p.layer("vector", lambda v: v.memset(ones_row128[:, :], 1.0), waits=(t_ld,))
    p.layer("vector", lambda v: v.memset(sc1["six"][:, :], 6.0))
    p.layer("vector", lambda v: v.reciprocal(sc1["inv6"][:, :], sc1["six"][:, :]))
    p.layer("vector", lambda v: v.reciprocal(sc1["rgsx"][:, :], sc1["gsx"][:, :]))
    p.layer("vector", lambda v: v.reciprocal(sc1["rgsh"][:, :], sc1["gsh"][:, :]))
    ones_row = ones_row128[0:1, :]

    t_stage_prev = None
    t_xq_prev = None
    for half, srcx in ((0, xp), (1, xa)):
        for i in range(NT):
            t_xt = p.dmas("sync", lambda y, inc, i=i, srcx=srcx: inc(
                y.dma_start(out=xtile[:, :], in_=srcx[i * 128:(i + 1) * 128, :])),
                waits=(p.last["vector"], p.last["scalar"]), stream="ld")
            t_lnt = emit_ln(p, nc, stack, [xtile[:, :]], [xtile[:, :]],
                            f"ln{half}_{i}", waits=(t_xt,), dump=dumpD, scr=lnscr)
            if half == 1:
                if t_xq_prev is not None:
                    p.wait("vector", t_xq_prev)
                tqx = emit_nvfp4_quant(p, xtile[:, :], xq_tm[:, :], scdmp[:, 0:64],
                                       sc1["gsx"][:, 0:1], sc1["rgsx"][:, 0:1],
                                       sc1["inv6"][:, 0:1], ns, D, waits=(t_lnt,))

                def trs(y, inc, i=i):
                    for hc in range(8):
                        inc(y.dma_start_transpose(out=xqT[:, hc, i * 128:(i + 1) * 128],
                                                  in_=xq_tm[:, hc * 128:(hc + 1) * 128]))

                t_xq_prev = p.dmas("sync", trs, waits=(tqx,), stream="xq")
            if t_stage_prev is not None:
                p.wait("vector", t_stage_prev)
            th = p.layer("vector", lambda v: v.tensor_copy(hl_h[:, :], xtile[:, :]),
                         waits=(t_lnt,))
            tl = p.layer("vector", lambda v: v.tensor_tensor(hl_l[:, :], xtile[:, :],
                                                             hl_h[:, :], op=OP.subtract))

            def trsp(y, inc):
                for hc in range(8):
                    inc(y.dma_start_transpose(out=stage_h[:, hc, :],
                                              in_=hl_h[:, hc * 128:(hc + 1) * 128]))
                    inc(y.dma_start_transpose(out=stage_l[:, hc, :],
                                              in_=hl_l[:, hc * 128:(hc + 1) * 128]))

            t_tr = p.dmas("sync", trsp, waits=(tl, t_stage_prev), stream="tr")

            def spl(y, inc, i=i, half=half):
                c0 = half * TH + i * 128
                dst_h = xnfm_h[:, c0:c0 + 128].rearrange("(c p) w -> p c w", p=128)
                dst_l = xnfm_l[:, c0:c0 + 128].rearrange("(c p) w -> p c w", p=128)
                inc(y.dma_start(out=dst_h, in_=stage_h[:, :, :]))
                inc(y.dma_start(out=dst_l, in_=stage_l[:, :, :]))

            t_stage_prev = p.dmas("sync", spl, waits=(t_tr,), stream="sp")
    t_xnfm = t_stage_prev
    t_xqT = t_xq_prev

    A.reset(m_base)
    p.phase_barrier()

    ldbuf = A.take((128, I0), F32)
    qbuf = A.take((128, I0), BF16)
    scdump = A.take((128, I0 // 16), F32)
    ns = NVScratch2(A)
    rhs_g = A.take((128, 8, 512), BF16)
    rhs_u = A.take((128, 8, 512), BF16)
    dnT = A.take((128, 4, D), BF16)
    hid = A.take((128, 512), F32)
    sil = A.take((128, 512), F32)
    hq_tm = A.take((128, 512), BF16)
    hqT = A.take((128, 4, 128), BF16)
    hsc = A.take((128, 32), F32)
    dacc = A.take((128, 4, D), F32)
    g_sb = A.take((128, D), F32)

    emit_global_absmax(p, nc, stack, gw, I0, D, ident[:, :], ones_row,
                       sc1["wm_g"][:, :], "g", ldbuf[:, 0:D], ps[7])
    emit_global_absmax(p, nc, stack, uw, I0, D, ident[:, :], ones_row,
                       sc1["wm_u"][:, :], "u", ldbuf[:, 0:D], ps[7])
    emit_global_absmax(p, nc, stack, dwn, D, I0, ident[:, :], ones_row,
                       sc1["wm_d"][:, :], "d", ldbuf[:, :], ps[7])
    for w, g_, rg_ in (("wm_g", "gs_g", "rgs_g"), ("wm_u", "gs_u", "rgs_u"), ("wm_d", "gs_d", "rgs_d")):
        p.layer("vector", lambda v, w=w, g_=g_: v.reciprocal(sc1[g_][:, :], sc1[w][:, :]))
        p.layer("vector", lambda v, g_=g_: v.tensor_scalar(sc1[g_][:, :], sc1[g_][:, :], 2688.0, None, op0=OP.mult))
        p.layer("vector", lambda v, g_=g_, rg_=rg_: v.reciprocal(sc1[rg_][:, :], sc1[g_][:, :]))
    p.layer("vector", lambda v: v.tensor_tensor(sc1["s1"][:, :], sc1["rgsx"][:, :], sc1["rgs_g"][:, :], op=OP.mult))
    p.layer("vector", lambda v: v.tensor_tensor(sc1["s2"][:, :], sc1["rgsx"][:, :], sc1["rgs_u"][:, :], op=OP.mult))
    p.layer("vector", lambda v: v.tensor_tensor(sc1["sd"][:, :], sc1["rgsh"][:, :], sc1["rgs_d"][:, :], op=OP.mult))

    emit_wquant(p, nc, gw, gq, I0, D, sc1["gs_g"][:, 0:1], sc1["rgs_g"][:, 0:1],
                sc1["inv6"][:, 0:1], ns, ldbuf[:, 0:D], qbuf[:, 0:D], scdump[:, 0:64])
    emit_wquant(p, nc, uw, uq, I0, D, sc1["gs_u"][:, 0:1], sc1["rgs_u"][:, 0:1],
                sc1["inv6"][:, 0:1], ns, ldbuf[:, 0:D], qbuf[:, 0:D], scdump[:, 0:64])
    t_wq = emit_wquant(p, nc, dwn, dq2, D, I0, sc1["gs_d"][:, 0:1], sc1["rgs_d"][:, 0:1],
                       sc1["inv6"][:, 0:1], ns, ldbuf[:, :], qbuf[:, :], scdump[:, :])

    gdst = out if stage == "ffn" else g_hbm
    t_go = None
    for grp in range(4):
        tiles = list(range(grp * 4, grp * 4 + 4))
        for i in range(4):
            p.layer("vector", lambda v, i=i: v.memset(dacc[:, i, :], 0.0),
                    waits=(p.last["scalar"],))
        t_mm_d = None
        for ib in range(8):
            def ldrhs(y, inc, ib=ib):
                for hc in range(8):
                    for k in range(4):
                        inc(y.dma_start_transpose(
                            out=rhs_g[:, hc, k * 128:(k + 1) * 128],
                            in_=gq[ib * 512 + k * 128: ib * 512 + (k + 1) * 128,
                                   hc * 128:(hc + 1) * 128]))
                        inc(y.dma_start_transpose(
                            out=rhs_u[:, hc, k * 128:(k + 1) * 128],
                            in_=uq[ib * 512 + k * 128: ib * 512 + (k + 1) * 128,
                                   hc * 128:(hc + 1) * 128]))
                for k in range(4):
                    inc(y.dma_start_transpose(
                        out=dnT[:, k, :],
                        in_=dq2[:, ib * 512 + k * 128: ib * 512 + (k + 1) * 128]))

            t_rhs = p.dmas("sync", ldrhs, waits=(t_wq, t_xqT, p.last["tensor"]), stream="w")
            for i in tiles:
                def mm_g(t, i=i):
                    o = None
                    for hc in range(8):
                        o = t.matmul(ps[0][:, :], xqT[:, hc, i * 128:(i + 1) * 128],
                                     rhs_g[:, hc, :], start=(hc == 0), stop=(hc == 7))
                    return o

                def mm_u(t, i=i):
                    o = None
                    for hc in range(8):
                        o = t.matmul(ps[1][:, :], xqT[:, hc, i * 128:(i + 1) * 128],
                                     rhs_u[:, hc, :], start=(hc == 0), stop=(hc == 7))
                    return o

                t_mg = p.layer("tensor", mm_g, waits=(t_rhs, p.last["scalar"]))
                t_mu = p.layer("tensor", mm_u, waits=(p.last["vector"],))
                t_sil = p.layer("scalar", lambda s: s.activation(
                    sil[:, :], ps[0][:, :], ACT.Silu, scale=sc1["s1"][:, 0:1]),
                    waits=(t_mg, p.last["vector"]))
                t_hid = p.layer("vector", lambda v: v.scalar_tensor_tensor(
                    hid[:, :], ps[1][:, :], sc1["s2"][:, 0:1], sil[:, :], op0=OP.mult, op1=OP.mult),
                    waits=(t_mu, t_sil))
                tq = emit_nvfp4_quant(p, hid[:, :], hq_tm[:, :], hsc[:, :],
                                      sc1["gsh"][:, 0:1], sc1["rgsh"][:, 0:1],
                                      sc1["inv6"][:, 0:1], ns, 512, waits=(t_hid,))

                def trh(y, inc):
                    for k in range(4):
                        inc(y.dma_start_transpose(out=hqT[:, k, :],
                                                  in_=hq_tm[:, k * 128:(k + 1) * 128]))

                t_trh = p.dmas("sync", trh, waits=(tq, t_mm_d), stream="tr")
                p.wait("vector", t_trh)

                def mm_d(t, i=i):
                    o = None
                    for dc in range(2):
                        for k in range(4):
                            o = t.matmul(ps[2 + dc][:, :], hqT[:, k, :],
                                         dnT[:, k, dc * 512:(dc + 1) * 512],
                                         start=(k == 0), stop=(k == 3))
                    return o

                t_mm_d = p.layer("tensor", mm_d, waits=(t_trh, p.last["vector"]))
                p.layer("vector", lambda v, i=i: v.tensor_tensor(
                    dacc[:, i - grp * 4, 0:512], dacc[:, i - grp * 4, 0:512], ps[2][:, :], op=OP.add),
                    waits=(t_mm_d,))
                p.layer("vector", lambda v, i=i: v.tensor_tensor(
                    dacc[:, i - grp * 4, 512:1024], dacc[:, i - grp * 4, 512:1024], ps[3][:, :], op=OP.add))
        for i in tiles:
            t_gs = p.layer("scalar", lambda s, i=i: s.activation(
                g_sb[:, :], dacc[:, i - grp * 4, :], ACT.Silu, scale=sc1["sd"][:, 0:1]),
                waits=(p.last["vector"],))
            t_go = p.dmas("sync", lambda y, inc, i=i: inc(
                y.dma_start(out=gdst[i * 128:(i + 1) * 128, :], in_=g_sb[:, :])), waits=(t_gs,), stream="sp")
            p.wait("scalar", t_go)

    if stage == "ffn":
        p.wait("sync", t_go)
        stack.close()
        return nc

    A.reset(m_base)
    p.phase_barrier()

    Kf = A.take((128, T), F32)
    Vf = A.take((128, T), F32)
    Qh = A.take((128, TH), F32)
    sqQ = A.take((128, TH), F32)
    skb = A.take((128, 1024), F32)
    As_fm = A.take((128, TH), F32)
    As_h = A.take((128, TH), BF16)
    As_l = A.take((128, TH), BF16)
    wsl = A.take((128, 8, 128), F32)
    wsl_h = {m: A.take((128, 8, 128), BF16) for m in "qkv"}
    wsl_l = {m: A.take((128, 8, 128), BF16) for m in "qkv"}
    xs_h = A.take((128, 512), BF16)
    xs_l = A.take((128, 512), BF16)
    scs = {k: A.take((128, 4), F32) for k in
           ("rdq", "den", "bcol")}
    sml = {k: A.take((128, 128), F32) for k in
           ("P", "am", "as_", "wneg", "pt", "vt", "skt", "sm1", "sm2", "dmm")}
    M_sb = A.take((128, 128), F32)
    z_sb = A.take((128, 2), F32)
    bmask = A.take((128, 128), F32)
    hmask = A.take((128, 2), F32)
    t_as_spill = None
    t_bm = p.dmas("sync", lambda y, inc: [inc(y.dma_start(out=bmask[:, :], in_=cin["bmask"][:, :])),
                                          inc(y.dma_start(out=hmask[:, :], in_=cin["hmask"][:, :]))],
                  stream="ld")
    p.wait("tensor", t_bm)
    p.wait("vector", t_bm)

    for pk in range(N_PACKS):
        c0 = pk * 128

        for m, wt in (("q", wqT), ("k", wkT), ("v", wvT)):
            def ldw1(y, inc, wt=wt):
                for ic in range(8):
                    inc(y.dma_start(out=wsl[:, ic, :],
                                    in_=wt[ic * 128:(ic + 1) * 128, c0:c0 + 128]))

            t_w = p.dmas("sync", ldw1, waits=(p.last["vector"],), stream="w")
            th = p.layer("vector", lambda v, m=m: v.tensor_copy(
                wsl_h[m][:, :, :], wsl[:, :, :]), waits=(t_w,))
            p.layer("vector", lambda v, m=m: v.tensor_tensor(
                wsl_l[m][:, :, :], wsl[:, :, :], wsl_h[m][:, :, :], op=OP.subtract))

        for tb in range(8):
            hbm_c0 = tb * 512
            for m, dst, lim in (("k", Kf, T), ("v", Vf, T), ("q", Qh, TH)):
                if m == "q" and hbm_c0 >= TH:
                    continue
                dst_c0 = hbm_c0
                src_c0 = (TH + hbm_c0) if (m == "q") else hbm_c0
                tacc = None
                for ic in range(8):
                    def ldx(y, inc, ic=ic, src_c0=src_c0):
                        inc(y.dma_start(out=xs_h[:, :], in_=xnfm_hbm_blk(xnfm_h, ic, src_c0)))
                        inc(y.dma_start(out=xs_l[:, :], in_=xnfm_hbm_blk(xnfm_l, ic, src_c0)))

                    t_x = p.dmas("sync", ldx, waits=(p.last["tensor"],), stream="ld")

                    def mm3(t, m=m, ic=ic):
                        t.matmul(ps[0][:, :], wsl_h[m][:, ic, :], xs_h[:, :],
                                 start=(ic == 0), stop=False)
                        t.matmul(ps[0][:, :], wsl_h[m][:, ic, :], xs_l[:, :],
                                 start=False, stop=False)
                        o = t.matmul(ps[0][:, :], wsl_l[m][:, ic, :], xs_h[:, :],
                                     start=False, stop=(ic == 7))
                        return o

                    tacc = p.layer("tensor", mm3,
                                   waits=(t_x,) + ((p.last["vector"],) if ic == 0 else ()))
                p.layer("vector", lambda v, dst=dst, dst_c0=dst_c0: v.tensor_copy(
                    dst[:, dst_c0:dst_c0 + 512], ps[0][:, :]), waits=(tacc,))

        p.layer("vector", lambda v: v.tensor_scalar(
            Kf[:, 0:TH], Kf[:, 0:TH], sc1["jf"][:, 0:1], None, op0=OP.mult))
        p.layer("vector", lambda v: v.tensor_scalar(
            sml["sm1"][:, 0:1], sc1["jf"][:, 0:1], 1e30, -1e30, op0=OP.mult, op1=OP.add))
        t_kmask = p.layer("vector", lambda v: v.tensor_scalar(
            Kf[:, 0:TH], Kf[:, 0:TH], sml["sm1"][:, 0:1], None, op0=OP.add))

        p.layer("vector", lambda v: v.tensor_scalar(sqQ[:, :], Qh[:, :], 0.0, None, op0=OP.min))
        t_e = p.layer("scalar", lambda s: s.activation(sqQ[:, :], sqQ[:, :], ACT.Exp),
                      waits=(p.last["vector"],))
        p.layer("vector", lambda v: v.scalar_tensor_tensor(
            sqQ[:, :], Qh[:, :], 0.0, sqQ[:, :], op0=OP.max, op1=OP.add), waits=(t_e,))

        t_b0 = p.layer("scalar", lambda s: s.activation(scs["bcol"][:, 0:1],
                                                        beta[:, 2 * pk:2 * pk + 1], ACT.Sigmoid))
        t_b1 = p.layer("scalar", lambda s: s.activation(scs["bcol"][:, 1:2],
                                                        beta[:, 2 * pk + 1:2 * pk + 2], ACT.Sigmoid))
        p.layer("vector", lambda v: v.memset(M_sb[:, :], 0.0), waits=(t_b1, p.last["tensor"]))
        p.layer("vector", lambda v: v.memset(z_sb[:, :], 0.0))

        emit_l0_scan(p, nc, ps, pk, Kf, Vf, Qh, sqQ, skb, As_fm, M_sb, z_sb, scs, sml,
                     sc1, causal_sb, ident, bmask, hmask)

        th = p.layer("vector", lambda v: v.tensor_copy(As_h[:, :], As_fm[:, :]))
        tl = p.layer("vector", lambda v: v.tensor_tensor(As_l[:, :], As_fm[:, :],
                                                         As_h[:, :], op=OP.subtract))

        def spas(y, inc, pk=pk):
            inc(y.dma_start(out=as_h[pk * 128:(pk + 1) * 128, :], in_=As_h[:, :]))
            inc(y.dma_start(out=as_l[pk * 128:(pk + 1) * 128, :], in_=As_l[:, :]))

        t_as_spill = p.dmas("sync", spas, waits=(tl,), stream="sp")
        p.wait("vector", t_as_spill)
        p.wait("tensor", t_as_spill)

    A.reset(m_base)
    p.phase_barrier()

    wo_h = A.take((128, 8, D), BF16)
    wo_l = A.take((128, 8, D), BF16)
    wld = A.take((128, D), F32)
    asld_h = A.take((128, 8, 128), BF16)
    asld_l = A.take((128, 8, 128), BF16)
    xg = A.take((128, D), F32)
    gg = A.take((128, D), F32)
    s_sb = A.take((128, D), F32)

    t_wo_prev = None
    for ic in range(8):
        t_w = p.dmas("sync", lambda y, inc, ic=ic: inc(
            y.dma_start(out=wld[:, :], in_=woT[ic * 128:(ic + 1) * 128, :])),
            waits=(p.last["vector"],), stream="w")
        p.layer("vector", lambda v, ic=ic: v.tensor_copy(wo_h[:, ic, :], wld[:, :]), waits=(t_w,))
        t_wo_prev = p.layer("vector", lambda v, ic=ic: v.tensor_tensor(
            wo_l[:, ic, :], wld[:, :], wo_h[:, ic, :], op=OP.subtract))

    for i in range(NT):
        def ldas(y, inc, i=i):
            for pc in range(8):
                inc(y.dma_start(out=asld_h[:, pc, :],
                                in_=as_h[pc * 128:(pc + 1) * 128, i * 128:(i + 1) * 128]))
                inc(y.dma_start(out=asld_l[:, pc, :],
                                in_=as_l[pc * 128:(pc + 1) * 128, i * 128:(i + 1) * 128]))
            inc(y.dma_start(out=xg[:, :], in_=xa[i * 128:(i + 1) * 128, :]))
            inc(y.dma_start(out=gg[:, :], in_=g_hbm[i * 128:(i + 1) * 128, :]))

        t_as = p.dmas("sync", ldas, waits=(t_wo_prev, t_as_spill, p.last["vector"], p.last["tensor"]), stream="ld")

        def mmo(t, i=i):
            o = None
            for dc in range(2):
                for pc in range(8):
                    o = t.matmul(ps[dc][:, :], asld_h[:, pc, :], wo_h[:, pc, dc * 512:(dc + 1) * 512],
                                 start=(pc == 0), stop=False)
                    o = t.matmul(ps[dc][:, :], asld_h[:, pc, :], wo_l[:, pc, dc * 512:(dc + 1) * 512],
                                 start=False, stop=False)
                    o = t.matmul(ps[dc][:, :], asld_l[:, pc, :], wo_h[:, pc, dc * 512:(dc + 1) * 512],
                                 start=False, stop=(pc == 7))
            return o

        t_mo = p.layer("tensor", mmo, waits=(t_as, p.last["vector"]))
        p.layer("vector", lambda v: v.tensor_copy(s_sb[:, 0:512], ps[0][:, :]), waits=(t_mo,))
        p.layer("vector", lambda v: v.tensor_copy(s_sb[:, 512:1024], ps[1][:, :]))
        p.layer("vector", lambda v, i=i: v.tensor_tensor(s_sb[:, :], s_sb[:, :], gg[:, :], op=OP.mult))
        t_x1 = p.layer("vector", lambda v, i=i: v.tensor_tensor(
            x1_t[:, i, :], xg[:, :], s_sb[:, :], op=OP.add))

    if stage == "l0":
        t_st = p.dmas("sync", lambda y, inc: [inc(y.dma_start(out=out[i * 128:(i + 1) * 128, :],
                                                              in_=x1_t[:, i, :])) for i in range(NT)],
                      waits=(p.last["vector"],), stream="sp")
        p.wait("sync", t_st)
        stack.close()
        return nc

    emit_l1(p, nc, stack, A, (m_persist, m_base), ps, sc1, cin, x1_t, ident, identb,
            ones_row128, piota_sb, out, stage=stage,
            dram=dict(xn1_hbm=nc.dram_tensor("xn1_hbm", [TH, D], F32).ap(),
                      x1_hbm=nc.dram_tensor("x1_hbm", [TH, D], F32).ap(),
                      s1_hbm=nc.dram_tensor("s1_hbm", [TH, D], F32).ap(),
                      gtab=nc.dram_tensor("gtab", [T, 640], BF16).ap(),
                      iqp_hbm=nc.dram_tensor("iqp_hbm", [256, TH], F32).ap(),
                      qc2_hbm=nc.dram_tensor("qc2_hbm", [512, TH], F32).ap(),
                      spq_hbm=nc.dram_tensor("spq_hbm", [256, TH], F32).ap(),
                      xnp=nc.dram_tensor("xnp", [T, D], F32).ap(),
                      cc_in=nc.dram_tensor("cc_in", [128, 3 * TH], F32),
                      cc_out=nc.dram_tensor("cc_out", [256, 3 * TH], F32),
                      cnt_in=nc.dram_tensor("cnt_in", [1, 8], F32),
                      cnt_out=nc.dram_tensor("cnt_out", [8, 8], F32, addr_space="Shared"),
                      cpr_out=nc.dram_tensor("cpr_out", [2, 8], F32),
                      ord_sh=nc.dram_tensor("ord_sh", [7296, 64], F32),
                      ord_red=nc.dram_tensor("ord_red", [7296, 64], F32),
                      eout_loc=nc.dram_tensor("eout_loc", [NSLOT, D], F32),
                      eout_pair=nc.dram_tensor("eout_pair", [2 * NSLOT, D], F32),
                      wq1=dict((m, nc.dram_tensor(f"l1w_{m}", list(sh), F32, kind="ExternalInput").ap())
                               for m, sh in L1W_SHAPES.items())))
    stack.close()
    return nc


def out_hbm_block(hbm, hc, c0):
    return hbm[hc * 128:(hc + 1) * 128, c0:c0 + 128]


def xnfm_hbm_blk(hbm, ic, c0):
    return hbm[ic * 128:(ic + 1) * 128, c0:c0 + 512]


def emit_l0_scan(p, nc, ps, pk, Kf, Vf, Qh, sqQ, skb, As_fm, M_sb, z_sb, scs, sml,
                 sc1, causal_sb, ident, bmask, hmask):
    ps_am, ps_s, ps_ad, ps_vp, ps_den, ps_pt, ps_vt, ps_skt = ps
    for s in range(N_STEPS):
        own = s >= 32
        kc = s * 64
        oc = (s - 32) * 64
        if s % 16 == 0:
            p.layer("vector", lambda v, kc=kc: v.tensor_scalar(
                skb[:, :], Kf[:, kc:kc + 1024], 0.0, None, op0=OP.min),
                waits=(p.last["scalar"],))
            t_e = p.layer("scalar", lambda s_: s_.activation(skb[:, :], skb[:, :], ACT.Exp),
                          waits=(p.last["vector"],))
            p.layer("vector", lambda v, kc=kc: v.scalar_tensor_tensor(
                skb[:, :], Kf[:, kc:kc + 1024], 0.0, skb[:, :], op0=OP.max, op1=OP.add),
                waits=(t_e,))
        sk = skb[:, (s % 16) * 64:(s % 16) * 64 + 64]

        t_vt = p.layer("tensor", lambda t, kc=kc: t.transpose(
            ps_vt[0:64, 0:128], Vf[:, kc:kc + 64], ident[:, :]), waits=(p.last["vector"],))
        t_vte = p.layer("vector", lambda v: v.tensor_copy(sml["vt"][0:64, :], ps_vt[0:64, 0:128]),
                        waits=(t_vt,))

        if own:
            sq = sqQ[:, oc:oc + 64]
            q = Qh[:, oc:oc + 64]

            def mm_a(t):
                t.matmul(ps_den[0:64, 0:2], sq, z_sb[:, :], start=True, stop=True,
                         skip_group_check=True)
                t.matmul(ps_am[0:64, 0:128], sq, M_sb[:, :], start=True, stop=True,
                         skip_group_check=True)
                t.matmul(ps_s[0:64, 0:64], q[0:64, :], Kf[0:64, kc:kc + 64],
                         start=True, stop=True, tile_position=(0, 0), skip_group_check=True)
                o = t.matmul(ps_s[64:128, 0:64], q[64:128, :], Kf[64:128, kc:kc + 64],
                             start=True, stop=True, tile_position=(64, 64), skip_group_check=True)
                return o

            t_a = p.layer("tensor", mm_a, waits=(p.last["vector"],))
            p.layer("vector", lambda v: v.tensor_tensor(sml["P"][:, 0:64], ps_s[:, 0:64],
                                                        causal_sb[:, :], op=OP.add), waits=(t_a,))
            p.layer("vector", lambda v: v.tensor_reduce(sml["sm1"][:, 0:1], sml["P"][:, 0:64],
                                                        axis=AXX, op=OP.max))
            p.layer("vector", lambda v: v.tensor_scalar(sml["sm1"][:, 1:2], sml["sm1"][:, 0:1],
                                                        -0.125, None, op0=OP.mult))
            t_ex = p.layer("scalar", lambda s_: s_.activation(
                sml["P"][:, 0:64], sml["P"][:, 0:64], ACT.Exp, scale=0.125,
                bias=sml["sm1"][:, 1:2]), waits=(p.last["vector"],))
            p.layer("vector", lambda v: v.tensor_reduce(sml["sm1"][:, 2:3], sml["P"][:, 0:64],
                                                        axis=AXX, op=OP.add), waits=(t_ex,))
            p.layer("vector", lambda v: v.reciprocal(sml["sm1"][:, 3:4], sml["sm1"][:, 2:3]))
            t_P = p.layer("vector", lambda v: v.tensor_scalar(
                sml["P"][:, 0:64], sml["P"][:, 0:64], sml["sm1"][:, 3:4], None, op0=OP.mult))
            t_pt = p.layer("tensor", lambda t: t.transpose(
                ps_pt[0:64, 0:128], sml["P"][:, 0:64], ident[:, :]), waits=(t_P,))
            t_pte = p.layer("vector", lambda v: v.tensor_copy(sml["pt"][0:64, :],
                                                              ps_pt[0:64, 0:128]), waits=(t_pt,))

            def mm_ad(t):
                t.matmul(ps_ad[0:64, 0:64], sml["pt"][0:64, 0:64], sml["vt"][0:64, 0:64],
                         start=True, stop=True, skip_group_check=True)
                o = t.matmul(ps_ad[0:64, 64:128], sml["pt"][0:64, 64:128],
                             sml["vt"][0:64, 64:128], start=True, stop=True,
                             skip_group_check=True)
                return o

            t_ad = p.layer("tensor", mm_ad, waits=(t_pte, t_vte, p.last["vector"]))
            p.layer("vector", lambda v: v.tensor_scalar(scs["den"][0:64, 0:2], ps_den[0:64, 0:2],
                                                        1e-6, None, op0=OP.add), waits=(t_a,))
            p.layer("vector", lambda v: v.reciprocal(scs["rdq"][0:64, 0:2], scs["den"][0:64, 0:2]))
            for h in range(2):
                p.layer("vector", lambda v, h=h: v.tensor_scalar(
                    sml["am"][0:64, 64 * h:64 * h + 64], ps_am[0:64, 64 * h:64 * h + 64],
                    scs["rdq"][0:64, h:h + 1], None, op0=OP.mult))
            p.layer("vector", lambda v: v.tensor_tensor(sml["am"][0:64, :], sml["am"][0:64, :],
                                                        ps_ad[0:64, 0:128], op=OP.subtract),
                    waits=(t_ad,))
            for h in range(2):
                p.layer("vector", lambda v, h=h: v.scalar_tensor_tensor(
                    sml["as_"][0:64, 64 * h:64 * h + 64], sml["am"][0:64, 64 * h:64 * h + 64],
                    scs["bcol"][0:64, h:h + 1], ps_ad[0:64, 64 * h:64 * h + 64],
                    op0=OP.mult, op1=OP.add))
            t_ast = p.layer("tensor", lambda t: t.transpose(
                ps_am[0:128, 128:192], sml["as_"][0:64, 0:128], ident[0:64, 0:64]),
                waits=(p.last["vector"],))
            p.layer("vector", lambda v, oc=oc: v.tensor_copy(As_fm[:, oc:oc + 64],
                                                             ps_am[0:128, 128:192]),
                    waits=(t_ast,))

        t_skt = p.layer("tensor", lambda t: t.transpose(
            ps_skt[0:64, 0:128], sk, ident[:, :]), waits=(p.last["vector"],))
        t_skte = p.layer("vector", lambda v: v.tensor_copy(sml["skt"][0:64, :],
                                                           ps_skt[0:64, 0:128]), waits=(t_skt,))

        def mm_mk(t):
            t.matmul(ps_den[0:64, 2:4], sk, z_sb[:, :], start=True, stop=True,
                     skip_group_check=True)
            o = t.matmul(ps_vp[0:64, 0:128], sk, M_sb[:, :], start=True, stop=True,
                         skip_group_check=True)
            return o

        t_mk = p.layer("tensor", mm_mk, waits=(p.last["vector"],))
        p.layer("vector", lambda v: v.tensor_scalar(scs["den"][0:64, 2:4], ps_den[0:64, 2:4],
                                                    1e-6, None, op0=OP.add), waits=(t_mk,))
        t_rdk = p.layer("vector", lambda v: v.reciprocal(scs["rdq"][0:64, 2:4], scs["den"][0:64, 2:4]))
        for h in range(2):
            p.layer("vector", lambda v, h=h: v.scalar_tensor_tensor(
                sml["wneg"][0:64, 64 * h:64 * h + 64], ps_vp[0:64, 64 * h:64 * h + 64],
                scs["rdq"][0:64, 2 + h:3 + h], sml["vt"][0:64, 64 * h:64 * h + 64],
                op0=OP.mult, op1=OP.subtract), waits=(t_mk, t_rdk, t_vte))
        t_wn = p.last["vector"]

        def mm_dm(t):
            return t.matmul(ps_s[0:128, 128:256], sml["skt"][0:64, :], sml["wneg"][0:64, :],
                            start=True, stop=True, skip_group_check=True)

        t_dm = p.layer("tensor", mm_dm, waits=(t_wn, t_skte, p.last["vector"]))
        p.layer("vector", lambda v: v.tensor_tensor(sml["dmm"][:, :], ps_s[0:128, 128:256],
                                                    bmask[:, :], op=OP.mult), waits=(t_dm,))
        p.layer("vector", lambda v: v.tensor_tensor(M_sb[:, :], M_sb[:, :], sml["dmm"][:, :],
                                                    op=OP.subtract))
        p.layer("vector", lambda v: v.tensor_reduce(sml["sm2"][:, 0:1], sk, axis=AXX, op=OP.add))
        p.layer("vector", lambda v: v.scalar_tensor_tensor(
            z_sb[:, 0:1], sml["sm2"][:, 0:1], hmask[:, 0:1], z_sb[:, 0:1],
            op0=OP.mult, op1=OP.add))
        p.layer("vector", lambda v: v.scalar_tensor_tensor(
            z_sb[:, 1:2], sml["sm2"][:, 0:1], hmask[:, 1:2], z_sb[:, 1:2],
            op0=OP.mult, op1=OP.add))



def emit_split_w(p, nc, A, hbm, R, C, tag, ldbuf):
    nch = R // 128
    wh = A.take((128, nch, C), BF16)
    wl = A.take((128, nch, C), BF16)
    for ch in range(nch):
        t_w = p.dmas("sync", lambda y, inc, ch=ch: inc(
            y.dma_start(out=ldbuf[:, 0:C], in_=hbm[ch * 128:(ch + 1) * 128, :])),
            waits=(p.last["vector"],), stream="w")
        p.layer("vector", lambda v, ch=ch: v.tensor_copy(wh[:, ch, :], ldbuf[:, 0:C]), waits=(t_w,))
        p.layer("vector", lambda v, ch=ch: v.tensor_tensor(wl[:, ch, :], ldbuf[:, 0:C],
                                                           wh[:, ch, :], op=OP.subtract))
    return wh, wl


def emit_proj3(p, nc, ps, wh, wl, xh, xl, out_sb, ncin, M, N0, NBLK, tag, psb=0):
    for tb in range(NBLK):
        def mm(t, tb=tb):
            o = None
            for ic in range(ncin):
                o = t.matmul(ps[psb][0:M, 0:N0], wh[:, ic, :], xh[:, ic, tb * N0:(tb + 1) * N0],
                             start=(ic == 0), stop=False, skip_group_check=True)
                o = t.matmul(ps[psb][0:M, 0:N0], wh[:, ic, :], xl[:, ic, tb * N0:(tb + 1) * N0],
                             start=False, stop=False, skip_group_check=True)
                o = t.matmul(ps[psb][0:M, 0:N0], wl[:, ic, :], xh[:, ic, tb * N0:(tb + 1) * N0],
                             start=False, stop=(ic == ncin - 1), skip_group_check=True)
            return o

        t_mm = p.layer("tensor", mm, waits=(p.last["vector"],))
        p.layer("vector", lambda v, tb=tb: v.tensor_copy(out_sb[0:M, tb * N0:(tb + 1) * N0],
                                                         ps[psb][0:M, 0:N0]), waits=(t_mm,))
    return p.last["vector"]


def emit_l1(p, nc, stack, A, marks, ps, sc1, cin, x1_t, ident, identb, ones_row128,
            piota, out, dram, stage="full"):
    m_persist, m_base = marks
    xn1_hbm = dram["xn1_hbm"]
    x1_hbm = dram["x1_hbm"]
    s1_hbm = dram["s1_hbm"]
    gtab = dram["gtab"]
    W = dram["wq1"]
    SC1 = float(96.0 ** -0.5)

    def newphase(mark):
        A.reset(mark)
        p.phase_barrier()

    def early_out():
        p.phase_barrier()
        t_eo = p.dmas("sync", lambda y, inc: inc(y.dma_start(out=out[:, :], in_=x1_hbm[:, :])),
                      stream="sp")
        p.wait("sync", t_eo)
        p.wait("vector", t_eo)

    newphase(m_base)
    xnt = A.take((128, D), F32)
    lnscr = tuple(A.take((128, 4), F32) for _ in range(4))
    dumpD = A.take((128, D), F32)
    for i in range(NT):
        t_ln = emit_ln(p, nc, stack, [x1_t[:, i, :]], [xnt[:, :]], f"l1_{i}",
                       waits=(p.last["sync"],), dump=dumpD, scr=lnscr)
        t_tm = p.dmas("sync", lambda y, inc, i=i: inc(
            y.dma_start(out=xn1_hbm[i * 128:(i + 1) * 128, :], in_=xnt[:, :])),
            waits=(t_ln,), stream="sp")
        p.wait("vector", t_tm)
        p.wait("scalar", t_tm)
    t_x1sp = p.dmas("sync", lambda y, inc: [inc(y.dma_start(out=x1_hbm[i * 128:(i + 1) * 128, :],
                                                            in_=x1_t[:, i, :])) for i in range(NT)],
                    waits=(p.last["vector"],), stream="sp")
    p.wait("vector", t_x1sp)
    p.wait("tensor", t_x1sp)
    p.wait("scalar", t_x1sp)
    p.wait("sync", t_x1sp)

    if stage == "l1p4":
        early_out()
        return
    newphase(m_persist)
    l_t = A.take((128, NT, 8), F32)
    rnk_t = A.take((128, NT), F32)
    psl_t = A.take((128, NT), F32)
    ohT_t = A.take((128, NT, 128), F32)
    off8 = A.take((128, 1), F32)
    m_tab = A.mark()

    xn1h = A.take((128, 8, TH), BF16)
    xn1l = A.take((128, 8, TH), BF16)
    gwh = A.take((128, 8, 8), BF16)
    gwl = A.take((128, 8, 8), BF16)
    xnt = A.take((128, D), F32)
    hlh = A.take((128, D), BF16)
    hll = A.take((128, D), BF16)
    wld = A.take((128, 512), F32)
    LT_sb = A.take((128, 128), F32)
    iotaMB = A.take((128, 8), F32)
    riota = A.take((128, 128), F32)
    rsc = A.take((128, 8), F32)
    rsc2 = A.take((128, 8), F32)
    rcol = A.take((128, 4), F32)

    t_rc = p.dmas("sync", lambda y, inc: [
        inc(y.dma_start(out=LT_sb[:, :], in_=cin["LTinc"][:, :])),
        inc(y.dma_start(out=iotaMB[:, :], in_=cin["iotaMB"][:, :])),
        inc(y.dma_start(out=riota[:, :], in_=cin["riota"][:, :]))], stream="ld")
    p.layer("vector", lambda v: v.memset(off8[:, :], 0.0), waits=(t_rc,))
    for ch in range(8):
        t_w = p.dmas("sync", lambda y, inc, ch=ch: inc(
            y.dma_start(out=wld[:, 0:8], in_=W["gwT"][ch * 128:(ch + 1) * 128, :])),
            waits=(p.last["vector"],), stream="w")
        p.layer("vector", lambda v, ch=ch: v.tensor_copy(gwh[:, ch, :], wld[:, 0:8]), waits=(t_w,))
        p.layer("vector", lambda v, ch=ch: v.tensor_tensor(gwl[:, ch, :], wld[:, 0:8],
                                                           gwh[:, ch, :], op=OP.subtract))
    t_sp = None
    for i in range(NT):
        t_xt = p.dmas("sync", lambda y, inc, i=i: inc(
            y.dma_start(out=xnt[:, :], in_=xn1_hbm[i * 128:(i + 1) * 128, :])),
            waits=(p.last["vector"],), stream="ld")
        if t_sp is not None:
            p.wait("vector", t_sp)
        p.layer("vector", lambda v: v.tensor_copy(hlh[:, :], xnt[:, :]), waits=(t_xt,))
        p.layer("vector", lambda v: v.tensor_tensor(hll[:, :], xnt[:, :], hlh[:, :], op=OP.subtract))

        def trf(y, inc, i=i):
            for hc in range(8):
                inc(y.dma_start_transpose(out=xn1h[:, hc, i * 128:(i + 1) * 128],
                                          in_=hlh[:, hc * 128:(hc + 1) * 128]))
                inc(y.dma_start_transpose(out=xn1l[:, hc, i * 128:(i + 1) * 128],
                                          in_=hll[:, hc * 128:(hc + 1) * 128]))

        t_sp = p.dmas("sync", trf, waits=(p.last["vector"],), stream="tr")
        p.wait("tensor", t_sp)

        def mml(t, i=i):
            o = None
            for ic in range(8):
                o = t.matmul(ps[4][:, 0:8], xn1h[:, ic, i * 128:(i + 1) * 128], gwh[:, ic, :],
                             start=(ic == 0), stop=False, skip_group_check=True)
                o = t.matmul(ps[4][:, 0:8], xn1l[:, ic, i * 128:(i + 1) * 128], gwh[:, ic, :],
                             start=False, stop=False, skip_group_check=True)
                o = t.matmul(ps[4][:, 0:8], xn1h[:, ic, i * 128:(i + 1) * 128], gwl[:, ic, :],
                             start=False, stop=(ic == 7), skip_group_check=True)
            return o

        t_l = p.layer("tensor", mml, waits=(p.last["vector"],))
        p.layer("vector", lambda v, i=i: v.tensor_copy(l_t[:, i, :], ps[4][:, 0:8]), waits=(t_l,))
        lv = l_t[:, i, :]
        p.layer("vector", lambda v, lv=lv: v.tensor_reduce(rcol[:, 0:1], lv, axis=AXX, op=OP.max))
        p.layer("vector", lambda v: v.tensor_scalar(rcol[:, 1:2], rcol[:, 0:1], -1.0, None,
                                                    op0=OP.mult))
        t_ex = p.layer("scalar", lambda s, lv=lv: s.activation(rsc[:, :], lv, ACT.Exp,
                                                               bias=rcol[:, 1:2]),
                       waits=(p.last["vector"],))
        p.layer("vector", lambda v: v.tensor_reduce(rcol[:, 2:3], rsc[:, :], axis=AXX, op=OP.add),
                waits=(t_ex,))
        p.layer("vector", lambda v, i=i: v.reciprocal(psl_t[:, i:i + 1], rcol[:, 2:3]))
        p.layer("vector", lambda v, lv=lv: v.tensor_scalar(rsc[:, :], lv, rcol[:, 0:1], None,
                                                           op0=OP.is_equal))
        p.layer("vector", lambda v: v.tensor_tensor(rsc2[:, :], rsc[:, :], iotaMB[:, :], op=OP.mult))
        p.layer("vector", lambda v: v.tensor_scalar(rsc2[:, :], rsc2[:, :], 1.0e9, None, op0=OP.add))
        p.layer("vector", lambda v: v.tensor_reduce(rcol[:, 3:4], rsc2[:, :], axis=AXX, op=OP.min))
        t_oh = p.layer("vector", lambda v: v.tensor_scalar(rsc[:, :], riota[:, 0:8], rcol[:, 3:4],
                                                           None, op0=OP.is_equal))
        t_ot = p.layer("tensor", lambda t: t.transpose(ps[5][0:8, 0:128], rsc[:, :], ident[:, :]),
                       waits=(t_oh,))
        t_rk = p.layer("tensor", lambda t: t.matmul(ps[6][0:8, 0:128], rsc[:, :], LT_sb[:, :],
                                                    start=True, stop=True, skip_group_check=True))
        p.layer("vector", lambda v, i=i: v.tensor_copy(ohT_t[0:8, i, :], ps[5][0:8, 0:128]),
                waits=(t_ot,))
        t_rnk8 = p.layer("vector", lambda v: v.tensor_scalar(
            wld[0:8, 0:128], ps[6][0:8, 0:128], off8[0:8, 0:1], -1.0, op0=OP.add, op1=OP.add),
            waits=(t_rk,))
        p.layer("vector", lambda v: v.tensor_tensor(off8[0:8, 0:1], off8[0:8, 0:1],
                                                    ps[6][0:8, 127:128], op=OP.add))
        t_rt = p.layer("tensor", lambda t: t.transpose(ps[7][0:128, 0:8], wld[0:8, 0:128],
                                                       ident[0:8, 0:8]),
                       waits=(t_rnk8,))
        p.layer("vector", lambda v: v.tensor_tensor(rsc2[:, :], rsc[:, :], ps[7][0:128, 0:8],
                                                    op=OP.mult), waits=(t_rt,))
        p.layer("vector", lambda v, i=i: v.tensor_reduce(rnk_t[:, i:i + 1], rsc2[:, :],
                                                         axis=AXX, op=OP.add))
    t_cnt = p.dmas("sync", lambda y, inc: inc(
        y.dma_start(out=dram["cnt_in"].ap().rearrange("a e -> (a e) ()"), in_=off8[0:8, 0:1])),
        waits=(p.last["vector"],), stream="sp")
    p.wait("gpsimd", t_cnt)
    p.wait("gpsimd", p.dma_tok("sp"), p.dma_tok("ld"))

    if stage != "l1p5nc":
        p.layer("gpsimd", lambda g: [
            g.load_library(library_config.mlp),
            g.collective_compute("AllGather", mybir.AluOpType.bypass,
                                 replica_groups=[list(range(8))],
                                 ins=[dram["cnt_in"].ap().opt()],
                                 outs=[dram["cnt_out"].ap().opt()])][-1])
        p.layer("gpsimd", lambda g: g.collective_compute(
            "AllGather", mybir.AluOpType.bypass,
            replica_groups=[[0, 1], [2, 3], [4, 5], [6, 7]],
            ins=[dram["cnt_in"].ap().opt()], outs=[dram["cpr_out"].ap().opt()]))
        t_coll = p.layer("gpsimd", lambda g: g.collective_compute(
            "AllGather", mybir.AluOpType.bypass,
            replica_groups=[[0, 1], [2, 3], [4, 5], [6, 7]],
            ins=[xn1_hbm.opt()], outs=[dram["xnp"].opt()]))

    ckv_own = A.take((128, 2, TH), F32)
    ik64 = A.take((128, TH), F32)
    krp = A.take((128, TH), F32)
    ldb = A.take((128, 512), F32)
    stg = A.take((128, 1024), F32)
    cq_fm = A.take((128, 4, TH), F32)

    wdkv = emit_split_w(p, nc, A, W["dkv"], D, 256, "dkv", ldb)
    wdq = emit_split_w(p, nc, A, W["dq"], D, 512, "dq", ldb)
    wiq = emit_split_w(p, nc, A, W["iwq"], D, 256, "iwq", ldb)
    wik = emit_split_w(p, nc, A, W["iwk"], D, 64, "iwk", ldb)
    wkr = emit_split_w(p, nc, A, W["kr"], D, 32, "kr", ldb)

    for cc in range(2):
        emit_proj3(p, nc, ps, wdkv[0][:, :, cc * 128:(cc + 1) * 128],
                   wdkv[1][:, :, cc * 128:(cc + 1) * 128],
                   xn1h, xn1l, ckv_own[:, cc, :], 8, 128, 512, 4, "ckv")
    for cc in range(4):
        emit_proj3(p, nc, ps, wdq[0][:, :, cc * 128:(cc + 1) * 128],
                   wdq[1][:, :, cc * 128:(cc + 1) * 128],
                   xn1h, xn1l, cq_fm[:, cc, :], 8, 128, 512, 4, "cq")
    for hh in range(4):
        for tb in range(4):
            def mmq(t, hh=hh, tb=tb):
                o = None
                for ic in range(8):
                    o = t.matmul(ps[0][0:64, 0:512], wiq[0][:, ic, hh * 64:(hh + 1) * 64],
                                 xn1h[:, ic, tb * 512:(tb + 1) * 512], start=(ic == 0),
                                 stop=False, skip_group_check=True)
                    o = t.matmul(ps[0][0:64, 0:512], wiq[0][:, ic, hh * 64:(hh + 1) * 64],
                                 xn1l[:, ic, tb * 512:(tb + 1) * 512], start=False,
                                 stop=False, skip_group_check=True)
                    o = t.matmul(ps[0][0:64, 0:512], wiq[1][:, ic, hh * 64:(hh + 1) * 64],
                                 xn1h[:, ic, tb * 512:(tb + 1) * 512], start=False,
                                 stop=(ic == 7), skip_group_check=True)
                return o

            t_mm = p.layer("tensor", mmq, waits=(p.last["vector"], p.last["sync"]))
            t_ev = p.layer("vector", lambda v: v.tensor_copy(stg[0:64, 0:512], ps[0][0:64, 0:512]),
                           waits=(t_mm,))
            t_iq = p.dmas("sync", lambda y, inc, hh=hh, tb=tb: inc(
                y.dma_start(out=dram["iqp_hbm"][hh * 64:(hh + 1) * 64, tb * 512:(tb + 1) * 512],
                            in_=stg[0:64, 0:512])), waits=(t_ev,), stream="sp")
            p.wait("vector", t_iq)
    emit_proj3(p, nc, ps, wik[0], wik[1], xn1h, xn1l, ik64[:, :], 8, 64, 512, 4, "ik")
    emit_proj3(p, nc, ps, wkr[0], wkr[1], xn1h, xn1l, krp[:, :], 8, 32, 512, 4, "kr")

    wuq = A.take((128, 4, 512), F32)
    t_wq = p.dmas("sync", lambda y, inc: [inc(y.dma_start(out=wuq[:, c, :],
                                                          in_=W["uq"][c * 128:(c + 1) * 128, :]))
                                          for c in range(4)], stream="w")
    p.wait("tensor", t_wq)
    for h in range(8):
        for tb in range(4):
            def mmqc(t, h=h, tb=tb):
                o = None
                for c in range(4):
                    o = t.matmul(ps[0][0:64, 0:512], wuq[:, c, h * 64:(h + 1) * 64],
                                 cq_fm[:, c, tb * 512:(tb + 1) * 512], start=(c == 0),
                                 stop=(c == 3), skip_group_check=True)
                return o

            t_mm = p.layer("tensor", mmqc, waits=(p.last["vector"], p.last["sync"]))
            t_ev = p.layer("vector", lambda v: v.tensor_copy(stg[0:64, 0:512], ps[0][0:64, 0:512]),
                           waits=(t_mm,))
            t_qc = p.dmas("sync", lambda y, inc, h=h, tb=tb: inc(
                y.dma_start(out=dram["qc2_hbm"][(h // 2) * 128 + 64 * (h % 2):
                                                (h // 2) * 128 + 64 * (h % 2) + 64,
                                                tb * 512:(tb + 1) * 512],
                            in_=stg[0:64, 0:512])), waits=(t_ev,), stream="sp")
            p.wait("vector", t_qc)
    wqr = A.take((128, 4, 256), F32)
    t_wr = p.dmas("sync", lambda y, inc: [inc(y.dma_start(out=wqr[:, c, :],
                                                          in_=W["qr"][c * 128:(c + 1) * 128, :]))
                                          for c in range(4)], stream="w")
    p.wait("tensor", t_wr)
    for h in range(8):
        for tb in range(4):
            def mmqr(t, h=h, tb=tb):
                o = None
                for c in range(4):
                    o = t.matmul(ps[0][0:32, 0:512], wqr[:, c, h * 32:(h + 1) * 32],
                                 cq_fm[:, c, tb * 512:(tb + 1) * 512], start=(c == 0),
                                 stop=(c == 3), skip_group_check=True)
                return o

            t_mm = p.layer("tensor", mmqr, waits=(p.last["vector"], p.last["sync"]))
            t_ev = p.layer("vector", lambda v: v.tensor_copy(stg[0:32, 0:512], ps[0][0:32, 0:512]),
                           waits=(t_mm,))
            t_ab = p.layer("scalar", lambda s: s.activation(stg[0:32, 512:1024], stg[0:32, 0:512],
                                                            ACT.Abs), waits=(t_ev,))
            t_e2 = p.layer("scalar", lambda s: s.activation(stg[0:32, 512:1024], stg[0:32, 512:1024],
                                                            ACT.Exp, scale=-1.0))
            p.layer("vector", lambda v: v.tensor_scalar(stg[0:32, 512:1024], stg[0:32, 512:1024],
                                                        1.0, None, op0=OP.add), waits=(t_e2,))
            t_l2 = p.layer("scalar", lambda s: s.activation(stg[0:32, 512:1024], stg[0:32, 512:1024],
                                                            ACT.Ln), waits=(p.last["vector"],))
            t_r2 = p.layer("scalar", lambda s: s.activation(stg[0:32, 0:512], stg[0:32, 0:512],
                                                            ACT.Relu))
            t_sp2 = p.layer("vector", lambda v: v.tensor_tensor(stg[0:32, 0:512], stg[0:32, 0:512],
                                                                stg[0:32, 512:1024], op=OP.add),
                            waits=(t_l2, t_r2))
            t_sq = p.dmas("sync", lambda y, inc, h=h, tb=tb: inc(
                y.dma_start(out=dram["spq_hbm"][(h // 4) * 128 + 32 * (h % 4):
                                                (h // 4) * 128 + 32 * (h % 4) + 32,
                                                tb * 512:(tb + 1) * 512],
                            in_=stg[0:32, 0:512])), waits=(t_sp2,), stream="sp")
            p.wait("vector", t_sq)
            p.wait("scalar", t_sq)

    cin_ap = dram["cc_in"].ap()
    t_ci = p.dmas("sync", lambda y, inc: [
        inc(y.dma_start(out=cin_ap[:, 0:2 * TH].rearrange("p (c w) -> p c w", c=2),
                        in_=ckv_own[:, :, :])),
        inc(y.dma_start(out=cin_ap[0:64, 2 * TH:3 * TH], in_=ik64[0:64, :])),
        inc(y.dma_start(out=cin_ap[64:96, 2 * TH:3 * TH], in_=krp[0:32, :]))],
        waits=(p.last["vector"], p.last["sync"]), stream="sh")
    if stage != "l1p5nc":
        p.wait("gpsimd", t_ci)
        t_cc = p.layer("gpsimd", lambda g: g.collective_compute(
            "AllGather", mybir.AluOpType.bypass,
            replica_groups=[[0, 1], [2, 3], [4, 5], [6, 7]],
            ins=[dram["cc_in"].ap().opt()], outs=[dram["cc_out"].ap().opt()]))

    if stage in ("l1p5", "l1p5nc"):
        early_out()
        return
    newphase(m_tab)
    ckv_hi = A.take((128, 2, T), BF16)
    ckv_lo = A.take((128, 2, T), BF16)
    klr4 = A.take((128, T), F32)
    gidx = A.take((128, NT, 256), I16)
    gidx0 = A.take((128, 1024), I16)
    m_p6 = A.mark()
    k_i = A.take((128, T), F32)
    m_p6s = A.mark()
    tA = A.take((128, 2, 128), F32)
    tB = A.take((128, 2, 128), F32)
    tfu = A.take((128, 2, 128), F32)
    kiA = A.take((128, 128), F32)
    kiB = A.take((128, 128), F32)
    krow = A.take((128, 256), F32)
    rotb = A.take((128, 3, 128), F32)
    cosb = A.take((128, 128), F32)
    sinb = A.take((128, 128), F32)
    rep4 = A.take((128, 128), F32)
    sgn16 = A.take((128, 1), F32)
    gstage = A.take((128, 640), BF16)
    psT = ps[6]
    psR = ps[5]
    co = dram["cc_out"].ap()
    jf = sc1["jf"][:, 0:1]

    t_c6 = p.dmas("sync", lambda y, inc: [
        inc(y.dma_start(out=rep4[0:32, :], in_=cin["rep4"][:, :])),
        inc(y.dma_start(out=sgn16[:, :], in_=cin["sgn16"][:, :]))], stream="ld")
    p.wait("vector", t_c6)
    for kt in range(32):
        u0 = kt * 128
        lo = u0 < 2048
        kk0 = u0 if lo else u0 - 2048

        def ldc(y, inc, kk0=kk0, lo=lo, u0=u0):
            inc(y.dma_start(out=tA[:, 0, :], in_=co[0:128, kk0:kk0 + 128]))
            inc(y.dma_start(out=tA[:, 1, :], in_=co[0:128, TH + kk0:TH + kk0 + 128]))
            inc(y.dma_start(out=kiA[0:96, :], in_=co[0:96, 2 * TH + kk0:2 * TH + kk0 + 128]))
            if not lo:
                inc(y.dma_start(out=tB[:, 0, :], in_=co[128:256, kk0:kk0 + 128]))
                inc(y.dma_start(out=tB[:, 1, :], in_=co[128:256, TH + kk0:TH + kk0 + 128]))
                inc(y.dma_start(out=kiB[0:96, :], in_=co[128:224, 2 * TH + kk0:2 * TH + kk0 + 128]))
            inc(y.dma_start(out=cosb[0:32, :], in_=cin["cosK"][:, u0:u0 + 128]))
            inc(y.dma_start(out=sinb[0:32, :], in_=cin["sinK"][:, u0:u0 + 128]))

        t_ld = p.dmas("sync", ldc, waits=(p.last["vector"], p.last["tensor"], p.last["scalar"],
                                          p.dma_tok("sh")), stream="ld")
        if lo:
            p.layer("vector", lambda v: v.tensor_scalar(tfu[:, :, :], tA[:, :, :], jf, None,
                                                        op0=OP.mult), waits=(t_ld,))
            p.layer("vector", lambda v: v.tensor_scalar(kiA[0:96, :], kiA[0:96, :],
                                                        sc1["jf"][0:96, 0:1], None, op0=OP.mult))
        else:
            p.layer("vector", lambda v: v.tensor_tensor(tfu[:, :, :], tB[:, :, :], tA[:, :, :],
                                                        op=OP.subtract), waits=(t_ld,))
            p.layer("vector", lambda v: v.tensor_scalar(tfu[:, :, :], tfu[:, :, :], jf, None,
                                                        op0=OP.mult))
            p.layer("vector", lambda v: v.tensor_tensor(tfu[:, :, :], tfu[:, :, :], tA[:, :, :],
                                                        op=OP.add))
            p.layer("vector", lambda v: v.tensor_tensor(kiB[0:96, :], kiB[0:96, :], kiA[0:96, :],
                                                        op=OP.subtract))
            p.layer("vector", lambda v: v.tensor_scalar(kiB[0:96, :], kiB[0:96, :],
                                                        sc1["jf"][0:96, 0:1], None, op0=OP.mult))
            p.layer("vector", lambda v: v.tensor_tensor(kiA[0:96, :], kiB[0:96, :], kiA[0:96, :],
                                                        op=OP.add))
        t_sel = p.layer("vector", lambda v, u0=u0: v.tensor_copy(k_i[0:64, u0:u0 + 128],
                                                                 kiA[0:64, :]))
        t_sh = p.dmas("sync", lambda y, inc: inc(
            y.dma_start(out=krow[0:32, 0:128], in_=kiA[64:96, 0:128])),
            waits=(t_sel,), stream="sh")
        t_ab = p.layer("scalar", lambda s: s.activation(krow[0:32, 128:256], krow[0:32, 0:128],
                                                        ACT.Abs), waits=(t_sh,))
        t_e = p.layer("scalar", lambda s: s.activation(krow[0:32, 128:256], krow[0:32, 128:256],
                                                       ACT.Exp, scale=-1.0))
        p.layer("vector", lambda v: v.tensor_scalar(krow[0:32, 128:256], krow[0:32, 128:256],
                                                    1.0, None, op0=OP.add), waits=(t_e,))
        t_l2 = p.layer("scalar", lambda s: s.activation(krow[0:32, 128:256], krow[0:32, 128:256],
                                                        ACT.Ln), waits=(p.last["vector"],))
        t_r2 = p.layer("scalar", lambda s: s.activation(krow[0:32, 0:128], krow[0:32, 0:128],
                                                        ACT.Relu))
        t_mu = p.layer("vector", lambda v: v.tensor_tensor(krow[0:32, 0:128], krow[0:32, 0:128],
                                                           krow[0:32, 128:256], op=OP.add),
                       waits=(t_l2, t_r2))
        p.layer("vector", lambda v: v.tensor_tensor(rotb[0:32, 0, :], krow[0:32, 0:128],
                                                    cosb[0:32, :], op=OP.mult))
        t_B = p.layer("vector", lambda v: v.tensor_tensor(rotb[0:32, 1, :], krow[0:32, 0:128],
                                                          sinb[0:32, :], op=OP.mult))
        t_bx = p.dmas("sync", lambda y, inc: [
            inc(y.dma_start(out=rotb[0:16, 2, :], in_=rotb[16:32, 1, :])),
            inc(y.dma_start(out=rotb[16:32, 2, :], in_=rotb[0:16, 1, :]))],
            waits=(t_B,), stream="sh")
        t_rot = p.layer("vector", lambda v: v.scalar_tensor_tensor(
            rotb[0:32, 1, :], rotb[0:32, 2, :], sgn16[0:32, 0:1], rotb[0:32, 0, :],
            op0=OP.mult, op1=OP.add), waits=(t_bx,))
        t_rp = p.layer("tensor", lambda t: t.matmul(psR[:, 0:128], rep4[0:32, :],
                                                    rotb[0:32, 1, :], start=True, stop=True,
                                                    skip_group_check=True), waits=(t_rot,))
        p.layer("vector", lambda v, u0=u0: v.tensor_copy(klr4[:, u0:u0 + 128], psR[:, 0:128]),
                waits=(t_rp,))
        p.layer("vector", lambda v, u0=u0: v.tensor_copy(ckv_hi[:, :, u0:u0 + 128], tfu[:, :, :]))
        t_hl = p.layer("vector", lambda v, u0=u0: v.tensor_tensor(ckv_lo[:, :, u0:u0 + 128],
                                                                  tfu[:, :, :],
                                                                  ckv_hi[:, :, u0:u0 + 128],
                                                                  op=OP.subtract))
        def trc(t):
            t.transpose(psT[:, 0:128], tfu[:, 0, :], ident[:, :])
            return t.transpose(psT[:, 128:256], tfu[:, 1, :], ident[:, :])

        t_tr = p.layer("tensor", trc, waits=(t_hl,))
        p.layer("vector", lambda v: v.tensor_copy(gstage[:, 0:256], psT[:, 0:256]), waits=(t_tr,))
        p.layer("vector", lambda v: v.tensor_tensor(gstage[:, 256:512], psT[:, 0:256],
                                                    gstage[:, 0:256], op=OP.subtract))
        t_tr2 = p.layer("tensor", lambda t: t.transpose(psT[:, 256:288], krow[0:32, 0:128],
                                                        ident[0:32, 0:32]),
                        waits=(p.last["vector"],))
        p.layer("vector", lambda v: v.tensor_copy(gstage[:, 512:544], psT[:, 256:288]),
                waits=(t_tr2,))
        p.layer("vector", lambda v: v.tensor_tensor(gstage[:, 544:576], psT[:, 256:288],
                                                    gstage[:, 512:544], op=OP.subtract))
        p.layer("vector", lambda v: v.memset(gstage[:, 576:640], 0.0))
        t_gt = p.dmas("sync", lambda y, inc, kt=kt: inc(
            y.dma_start(out=gtab[kt * 128:(kt + 1) * 128, :], in_=gstage[:, :])),
            waits=(p.last["vector"],), stream="sp")
        p.wait("vector", t_gt)
        p.wait("tensor", t_gt)

    newphase(m_p6s)
    wstage = A.take((128, 8, 128), F32)
    wrf = A.take((128, 1024), F32)

    def wrap16(src_ap, ncol, nph, wrout, waits=()):
        t_sh = p.dmas("sync", lambda y, inc, ncol=ncol, nph=nph: [inc(y.dma_start(
            out=wstage[0:16, ph, 0:ncol], in_=src_ap[ph * 16:(ph + 1) * 16, 0:ncol]))
            for ph in range(nph)], waits=tuple(waits) + (p.last["vector"],), stream="sh")
        wrv = wrf[:, 0:ncol * nph].rearrange("p (c h) -> p c h", h=nph)
        t_pc = p.layer("vector", lambda v: v.tensor_copy(
            wrv[0:16, :, :], wstage[0:16, 0:nph, 0:ncol].rearrange("p a b -> p b a")),
            waits=(t_sh,))
        t_cv = p.layer("vector", lambda v: v.tensor_copy(wrout[0:16, :],
                                                         wrf[0:16, 0:ncol * nph]))
        t_rp = p.dmas("sync", lambda y, inc: [inc(y.dma_start(
            out=wrout[g * 16:(g + 1) * 16, :], in_=wrout[0:16, :])) for g in range(1, 8)],
            waits=(t_cv,), stream="sh")
        p.wait("vector", t_rp)
        return t_rp

    I_sb = A.take((128, T), F32)
    iqt = A.take((128, 4, 128), F32)
    idxw = A.take((128, 4), F32)
    rbuf = A.take((128, 512), F32)
    rbuf2 = A.take((128, 512), F32)
    cmask = A.take((128, 4, 128), F32)
    bmask2 = A.take((128, 4, 256), F32)
    vals8 = A.take((128, 8), F32)
    idx8 = A.take((128, 8), U32)
    exidx = A.take((128, 32), F32)
    glist = A.take((128, 128), F32)
    glist2 = A.take((128, 128), F32)
    svj = A.take((128, 1), F32)
    wst = A.take((128, 1), F32)
    riota = A.take((128, 64), F32)
    clampu = A.take((128, 1), F32)
    upos = A.take((128, NT), F32)

    t_cb = p.dmas("sync", lambda y, inc: [
        inc(y.dma_start(out=idxw[:, :], in_=W["idxw"][:, :])),
        inc(y.dma_start(out=cmask[:, 0, :], in_=cin["cm01"][:, :])),
        inc(y.dma_start(out=cmask[:, 1, :], in_=cin["cmneg"][:, :])),
        inc(y.dma_start(out=bmask2[:, 0, :], in_=cin["b01"][:, :])),
        inc(y.dma_start(out=bmask2[:, 1, :], in_=cin["bneg"][:, :])),
        inc(y.dma_start(out=bmask2[:, 2, :], in_=cin["b01t0"][:, :])),
        inc(y.dma_start(out=bmask2[:, 3, :], in_=cin["bnegt0"][:, :])),
        inc(y.dma_start(out=clampu[:, :], in_=cin["clampu"][:, :])),
        inc(y.dma_start(out=upos[:, :], in_=cin["upos"][:, :])),
        inc(y.dma_start(out=riota[:, :], in_=cin["riota"][:, 0:64]))], stream="ld")
    p.layer("vector", lambda v: v.tensor_scalar(svj[:, :], sc1["jf"][:, 0:1], 2.0e30, -2.0e30,
                                                op0=OP.mult, op1=OP.add), waits=(t_cb,))
    for i in range(NT):
        t0u = 2048 + i * 128
        Wt = t0u + 128
        t_qt = p.dmas("sync", lambda y, inc, i=i: [
            inc(y.dma_start(out=iqt[0:64, hh, :],
                            in_=dram["iqp_hbm"][hh * 64:(hh + 1) * 64, i * 128:(i + 1) * 128]))
            for hh in range(4)], waits=(p.last["tensor"],), stream="ld")
        nb = (Wt + 511) // 512
        for kb in range(nb):
            Nb = min(512, Wt - kb * 512)

            def mmi(t, kb=kb, Nb=Nb):
                o = None
                for hh in range(4):
                    o = t.matmul(ps[hh][0:128, 0:Nb], iqt[0:64, hh, :],
                                 k_i[0:64, kb * 512:kb * 512 + Nb], start=True, stop=True,
                                 skip_group_check=True)
                return o

            t_mm = p.layer("tensor", mmi, waits=(t_qt, p.last["vector"], p.last["scalar"]))
            t_r0 = p.layer("scalar", lambda s, kb=kb, Nb=Nb: s.activation(
                rbuf[:, 0:Nb], ps[0][0:128, 0:Nb], ACT.Relu), waits=(t_mm, p.last["vector"]))
            p.layer("vector", lambda v, kb=kb, Nb=Nb: v.tensor_scalar(
                I_sb[:, kb * 512:kb * 512 + Nb], rbuf[:, 0:Nb], idxw[:, 0:1], None,
                op0=OP.mult), waits=(t_r0,))
            for hh in range(1, 4):
                t_rh = p.layer("scalar", lambda s, hh=hh, Nb=Nb: s.activation(
                    rbuf2[:, 0:Nb], ps[hh][0:128, 0:Nb], ACT.Relu),
                    waits=(p.last["vector"],))
                p.layer("vector", lambda v, hh=hh, kb=kb, Nb=Nb: v.scalar_tensor_tensor(
                    I_sb[:, kb * 512:kb * 512 + Nb], rbuf2[:, 0:Nb], idxw[:, hh:hh + 1],
                    I_sb[:, kb * 512:kb * 512 + Nb], op0=OP.mult, op1=OP.add), waits=(t_rh,))
        p.layer("vector", lambda v: v.tensor_scalar(I_sb[:, 0:2048], I_sb[:, 0:2048],
                                                    sc1["jf"][:, 0:1], None, op0=OP.mult))
        p.layer("vector", lambda v: v.tensor_scalar(I_sb[:, 0:2048], I_sb[:, 0:2048],
                                                    svj[:, 0:1], None, op0=OP.add))
        p.layer("vector", lambda v, t0u=t0u: v.tensor_tensor(
            I_sb[:, t0u:t0u + 128], I_sb[:, t0u:t0u + 128], cmask[:, 0, :], op=OP.mult))
        p.layer("vector", lambda v, t0u=t0u: v.tensor_tensor(
            I_sb[:, t0u:t0u + 128], I_sb[:, t0u:t0u + 128], cmask[:, 1, :], op=OP.add))
        bsl = 2 if i == 0 else 0
        p.layer("vector", lambda v, t0u=t0u, bsl=bsl: v.tensor_tensor(
            I_sb[:, t0u - 128:t0u + 128], I_sb[:, t0u - 128:t0u + 128], bmask2[:, bsl, :],
            op=OP.mult))
        p.layer("vector", lambda v, t0u=t0u, bsl=bsl: v.tensor_tensor(
            I_sb[:, t0u - 128:t0u + 128], I_sb[:, t0u - 128:t0u + 128], bmask2[:, bsl + 1, :],
            op=OP.add))
        for rr in range(4):
            p.layer("vector", lambda v, Wt=Wt: v.max(vals8[:, :], I_sb[:, 0:Wt]))
            t_mx = p.layer("vector", lambda v, Wt=Wt: v.max_index(idx8[:, :], vals8[:, :],
                                                                  I_sb[:, 0:Wt]))
            p.layer("vector", lambda v, rr=rr: v.tensor_copy(exidx[:, rr * 8:(rr + 1) * 8],
                                                             idx8[:, :]))
            if rr < 3:
                p.layer("vector", lambda v, Wt=Wt: v.match_replace(
                    I_sb[:, 0:Wt], vals8[:, :], I_sb[:, 0:Wt], NEG_BIG))
        if i == 0:
            p.layer("vector", lambda v: v.tensor_scalar(wst[:, :], upos[:, 0:1], -63.0, None,
                                                        op0=OP.add))
            p.layer("vector", lambda v: v.tensor_tensor(wst[:, :], wst[:, :], clampu[:, :],
                                                        op=OP.max))
            p.layer("vector", lambda v: v.tensor_scalar(glist[:, 0:64], riota[:, 0:64],
                                                        wst[:, 0:1], None, op0=OP.add))
            p.layer("vector", lambda v: v.tensor_copy(glist[:, 64:96], exidx[:, :]))
            p.layer("vector", lambda v: v.memset(glist[:, 96:128], 0.0))
            p.layer("vector", lambda v: v.tensor_scalar(glist[:, 96:128], glist[:, 96:128],
                                                        upos[:, 0:1], None, op0=OP.add))
            Wg = 128
        else:
            p.layer("vector", lambda v: v.tensor_copy(glist[:, 0:32], exidx[:, :]))
            Wg = 32
        t_gt2 = p.layer("tensor", lambda t, Wg=Wg: t.transpose(
            ps[7][0:Wg, 0:128], glist[:, 0:Wg], ident[:, :]), waits=(p.last["vector"],))
        t_ge = p.layer("vector", lambda v, Wg=Wg: v.tensor_copy(glist2[0:Wg, :],
                                                                ps[7][0:Wg, 0:128]),
                       waits=(t_gt2,))
        wrout = gidx0[:, :] if i == 0 else gidx[:, i, :]
        wrap16(glist2, 128, Wg // 16, wrout, waits=(t_ge,))
    
    if stage == "l1p6":
        early_out()
        return
    newphase(m_p6)
    ukz = A.take((128, 8, 256), F32)
    wuvh = A.take((128, 2, 512), BF16)
    wuvl = A.take((128, 2, 512), BF16)
    bm01 = A.take((128, 8, 256), BF16)
    bmng = A.take((128, 8, 256), BF16)
    mck = A.take((128, 4, 128), BF16)
    cosRg = A.take((128, 2, 128), F32)
    cosR0 = A.take((128, 2, 128), F32)
    rep4 = A.take((128, 128), F32)
    sgn16 = A.take((128, 1), F32)
    ldb = A.take((128, 1024), F32)
    woh = A.take((128, 1024), BF16)
    wol = A.take((128, 1024), BF16)
    qct = A.take((128, 4, 128), F32)
    spqt = A.take((128, 2, 128), F32)
    ctab = A.take((128, 4, 128), F32)
    rotL = A.take((128, 2, 128), F32)
    rotS = A.take((128, 2, 128), F32)
    lhL = A.take((128, 2, 8, 16, 8), F32)
    lhS = A.take((128, 2, 8, 16, 8), F32)
    qth = A.take((128, 2, 8, 16, 8), BF16)
    qtl = A.take((128, 2, 8, 16, 8), BF16)
    tmbh = A.take((128, 2, 256), BF16)
    tmbl = A.take((128, 2, 256), BF16)
    gath = A.take((128, 2, 16, 640), BF16)
    scdh = A.take((128, 2, 512), BF16)
    scdl = A.take((128, 2, 512), BF16)
    mrot = A.take((128, 3, 128), F32)
    mus = A.take((128, 32), F32)
    klrg = A.take((128, 512), F32)
    rA = klrg[:, 0:256].rearrange("p (c w) -> p c w", c=2)
    rBx = klrg[:, 256:512].rearrange("p (c w) -> p c w", c=2)
    S_sb = A.take((128, 288), F32)
    dsel = A.take((128, 32, 16), BF16)
    dsel0 = A.take((128, 128, 4), BF16)
    Ph = A.take((128, 288), BF16)
    Pl = A.take((128, 288), BF16)
    PTbh = A.take((128, 2, 128), BF16)
    PTbl = A.take((128, 2, 128), BF16)
    PTsh = A.take((128, 128), BF16)
    PTsl = A.take((128, 128), BF16)
    lkh = A.take((128, 128), BF16)
    lkl = A.take((128, 128), BF16)
    AGG = A.take((128, 256), F32)
    AGTh = A.take((128, 2, 8, 128), BF16)
    AGTl = A.take((128, 2, 8, 128), BF16)
    hoth = A.take((128, 8, 128), BF16)
    hotl = A.take((128, 8, 128), BF16)
    s1sb = A.take((128, 1024), F32)
    smx = A.take((128, 4), F32)
    stg8 = ldb[:, 0:256]

    bm01b = nc.dram_tensor("bm01b", [128, 2048], BF16, kind="ExternalInput").ap()
    dselb_in = nc.dram_tensor("dselb", [128, 512], BF16, kind="ExternalInput").ap()
    dsel0b_in = nc.dram_tensor("dsel0b", [128, 2048], BF16, kind="ExternalInput").ap()
    bmngb = nc.dram_tensor("bmngb", [128, 2048], BF16, kind="ExternalInput").ap()
    mckb = nc.dram_tensor("mckb", [128, 512], BF16, kind="ExternalInput").ap()
    pb2 = ps[2][:, :].bitcast(BF16)
    t_p7 = p.dmas("sync", lambda y, inc: [
        inc(y.dma_start(out=ukz[:, :, :], in_=W["ukz"].rearrange("p (h c) -> p h c", h=8))),
        inc(y.dma_start(out=ldb[:, 0:512], in_=W["uv"][0:128, :])),
        inc(y.dma_start(out=bm01[:, :, :], in_=bm01b.rearrange("p (g c) -> p g c", g=8))),
        inc(y.dma_start(out=bmng[:, :, :], in_=bmngb.rearrange("p (g c) -> p g c", g=8))),
        inc(y.dma_start(out=mck[:, :, :], in_=mckb.rearrange("p (k c) -> p k c", k=4))),
        inc(y.dma_start(out=cosRg[0:32, 0, :], in_=cin["cosRg"][:, :])),
        inc(y.dma_start(out=cosRg[0:32, 1, :], in_=cin["sinRg"][:, :])),
        inc(y.dma_start(out=cosR0[0:32, 0, :], in_=cin["cosR0"][:, :])),
        inc(y.dma_start(out=cosR0[0:32, 1, :], in_=cin["sinR0"][:, :])),
        inc(y.dma_start(out=rep4[0:32, :], in_=cin["rep4"][:, :])),
        inc(y.dma_start(out=dsel[:, :, :], in_=dselb_in.rearrange("p (r q) -> p r q", r=32))),
        inc(y.dma_start(out=sgn16[:, :], in_=cin["sgn16"][:, :]))], stream="ld")
    p.layer("vector", lambda v: v.tensor_copy(wuvh[:, 0, :], ldb[:, 0:512]), waits=(t_p7,))
    p.layer("vector", lambda v: v.tensor_tensor(wuvl[:, 0, :], ldb[:, 0:512], wuvh[:, 0, :],
                                                op=OP.subtract))
    t_uv2 = p.dmas("sync", lambda y, inc: inc(
        y.dma_start(out=ldb[:, 0:512], in_=W["uv"][128:256, :])),
        waits=(p.last["vector"],), stream="ld")
    p.layer("vector", lambda v: v.tensor_copy(wuvh[:, 1, :], ldb[:, 0:512]), waits=(t_uv2,))
    p.layer("vector", lambda v: v.tensor_tensor(wuvl[:, 1, :], ldb[:, 0:512], wuvh[:, 1, :],
                                                op=OP.subtract))

    units = []
    for i_ in range(NT):
        units += [(i_, u_) for u_ in range(8 if i_ == 0 else 2)]
    unit_cid = []
    calls = []
    for i_ in range(NT):
        nun_ = 8 if i_ == 0 else 2
        if i_ >= 14:
            calls.append((len(unit_cid), 2))
            unit_cid += [len(calls) - 1] * 2
        else:
            for u_ in range(nun_):
                calls.append((len(unit_cid), 1))
                unit_cid.append(len(calls) - 1)

    gtoks = {}

    def issue_call(cid):
        fu, nu = calls[cid]
        i_, u0 = units[fu]
        if i_ == 0:
            iap = gidx0[:, u0 * 128:(u0 + nu) * 128]
        else:
            iap = gidx[:, i_, u0 * 128:(u0 + nu) * 128]
        if nu == 1:
            oap = gath[:, fu % 2, :, :]
        else:
            oap = gath[:, :, :, :].rearrange("p a b c -> p (a b) c")
        gtoks[cid] = p.dmas("gpsimd", lambda g, inc: inc(
            g.dma_gather(oap, gtab[:, :], iap, num_idxs=2048 * nu,
                         num_idxs_reg=2048 * nu, elem_size=640, single_packet=False)),
            waits=(p.last["tensor"], p.last["vector"]), stream="gx")

    issue_call(0)
    uix_glob = [0]

    for i in range(NT):
        tile0 = (i == 0)
        def ldt(y, inc, i=i):
            for hp in range(4):
                inc(y.dma_start(out=qct[:, hp, :],
                                in_=dram["qc2_hbm"][hp * 128:(hp + 1) * 128,
                                                    i * 128:(i + 1) * 128]))
            for c in range(2):
                inc(y.dma_start(out=spqt[:, c, :],
                                in_=dram["spq_hbm"][c * 128:(c + 1) * 128,
                                                    i * 128:(i + 1) * 128]))
            for k, nm in enumerate(("cQLs", "sQLs", "cQSs", "sQSs")):
                inc(y.dma_start(out=ctab[:, k, :], in_=cin[nm][:, i * 128:(i + 1) * 128]))

        t_lt = p.dmas("sync", ldt, waits=(p.last["tensor"], p.last["vector"]), stream="ld")
        for h in range(8):
            for dch in range(2):
                t_qm = p.layer("tensor", lambda t, h=h, dch=dch: t.matmul(
                    ps[5][:, 0:128], ukz[:, h, dch * 128:(dch + 1) * 128], qct[:, h // 2, :],
                    start=True, stop=True, skip_group_check=True),
                    waits=(t_lt, p.last["vector"]))
                p.layer("vector", lambda v, h=h, dch=dch: v.tensor_copy(
                    qth[:, dch, :, :, h],
                    ps[5][:, 0:128].rearrange("p (g q) -> p g q", g=8)), waits=(t_qm,))
                p.layer("vector", lambda v, h=h, dch=dch: v.tensor_tensor(
                    qtl[:, dch, :, :, h], ps[5][:, 0:128].rearrange("p (g q) -> p g q", g=8),
                    qth[:, dch, :, :, h], op=OP.subtract))
        for kind, rot, ci, si in ((0, rotL, 0, 1), (1, rotS, 2, 3)):
            if tile0 and kind == 0:
                continue
            p.layer("vector", lambda v, rot=rot, ci=ci: v.tensor_tensor(
                rA[:, 0, :], spqt[:, 0, :], ctab[:, ci, :], op=OP.mult))
            p.layer("vector", lambda v, rot=rot, ci=ci: v.tensor_tensor(
                rA[:, 1, :], spqt[:, 1, :], ctab[:, ci, :], op=OP.mult))
            p.layer("vector", lambda v, si=si: v.tensor_tensor(
                rBx[:, 0, :], spqt[:, 0, :], ctab[:, si, :], op=OP.mult))
            t_b = p.layer("vector", lambda v, si=si: v.tensor_tensor(
                rBx[:, 1, :], spqt[:, 1, :], ctab[:, si, :], op=OP.mult))
            def swp(y, inc, rot=rot):
                for a in range(4):
                    inc(y.dma_start(out=rot[a * 32:a * 32 + 16, :, :],
                                    in_=rBx[a * 32 + 16:a * 32 + 32, :, :]))
                    inc(y.dma_start(out=rot[a * 32 + 16:a * 32 + 32, :, :],
                                    in_=rBx[a * 32:a * 32 + 16, :, :]))

            t_sw = p.dmas("sync", swp, waits=(t_b,), stream="sh")
            p.layer("vector", lambda v, rot=rot: v.scalar_tensor_tensor(
                rot[:, :, :], rot[:, :, :], sgn16[:, 0:1], rA[:, :, :],
                op0=OP.mult, op1=OP.add), waits=(t_sw,))
        for kind, lh, rot in ((0, lhL, rotL), (1, lhS, rotS)):
            if tile0 and kind == 0:
                continue
            t_ms = p.layer("vector", lambda v, lh=lh: v.memset(lh[:, :, :, :, :], 0.0))
            for h in range(8):
                a, c = h % 4, h // 4
                p.layer("vector", lambda v, lh=lh, rot=rot, a=a, c=c, h=h: v.tensor_copy(
                    lh[a * 32:(a + 1) * 32, c, :, :, h],
                    rot[a * 32:(a + 1) * 32, c, :].rearrange("p (g q) -> p g q", g=8)))
        if not tile0:
            for bb, kt in enumerate((15 + i, 16 + i)):
                def trb(t, bb=bb, kt=kt):
                    t.transpose(pb2[:, 0:128], ckv_hi[:, 0, kt * 128:(kt + 1) * 128],
                                identb[:, :])
                    return t.transpose(pb2[:, 128:256], ckv_hi[:, 1, kt * 128:(kt + 1) * 128],
                                       identb[:, :])

                t_tb = p.layer("tensor", trb, waits=(p.last["vector"],))
                p.layer("vector", lambda v, bb=bb: v.tensor_copy(tmbh[:, bb, :], pb2[:, 0:256]),
                        waits=(t_tb,))

                def trbl(t, bb=bb, kt=kt):
                    t.transpose(pb2[:, 0:128], ckv_lo[:, 0, kt * 128:(kt + 1) * 128],
                                identb[:, :])
                    return t.transpose(pb2[:, 128:256], ckv_lo[:, 1, kt * 128:(kt + 1) * 128],
                                       identb[:, :])

                t_tb2 = p.layer("tensor", trbl, waits=(p.last["vector"],))
                p.layer("vector", lambda v, bb=bb: v.tensor_copy(tmbl[:, bb, :], pb2[:, 0:256]),
                        waits=(t_tb2,))
        nun = 8 if tile0 else 2
        for u in range(nun):
            uix = uix_glob[0] + u
            cid = unit_cid[uix]
            if cid not in gtoks:
                issue_call(cid)
            if uix + 1 < len(units):
                ncid = unit_cid[uix + 1]
                if ncid not in gtoks and calls[ncid][1] == 1 and calls[cid][1] == 1:
                    issue_call(ncid)
            p.wait("tensor", gtoks[cid])
            p.wait("vector", gtoks[cid])
            gbuf = gath[:, uix % 2, :, :] if calls[cid][1] == 1 else gath[:, u, :, :]
            groups = [u] if tile0 else ([0, 1, 2, 3] if u == 0 else [4, 5, 6, 7])
            for g in groups:
                nrd = 4 if tile0 else 1
                if not tile0:
                    gsl = (g % 4) * 4
                for rd in range(nrd):
                    for ckl in range(4):
                        sl = (u * 16 + rd * 4 + ckl) if tile0 else (gsl + ckl)
                        sl = sl % 16

                        def trs(t, sl=sl):
                            t.transpose(pb2[:, 0:128], gbuf[:, sl, 0:128], identb[:, :])
                            t.transpose(pb2[:, 128:256], gbuf[:, sl, 128:256], identb[:, :])
                            t.transpose(pb2[:, 256:384], gbuf[:, sl, 256:384], identb[:, :])
                            return t.transpose(pb2[:, 384:512], gbuf[:, sl, 384:512],
                                               identb[:, :])

                        t_ts = p.layer("tensor", trs, waits=(p.last["vector"],))
                        p.layer("vector", lambda v, ckl=ckl: v.tensor_copy(
                            scdh[:, 0, ckl * 128:(ckl + 1) * 128], pb2[:, 0:128]),
                            waits=(t_ts,))
                        p.layer("vector", lambda v, ckl=ckl: v.tensor_copy(
                            scdh[:, 1, ckl * 128:(ckl + 1) * 128], pb2[:, 128:256]))
                        p.layer("vector", lambda v, ckl=ckl: v.tensor_copy(
                            scdl[:, 0, ckl * 128:(ckl + 1) * 128], pb2[:, 256:384]))
                        p.layer("vector", lambda v, ckl=ckl: v.tensor_copy(
                            scdl[:, 1, ckl * 128:(ckl + 1) * 128], pb2[:, 384:512]))
                        t_mu2 = p.layer("vector", lambda v, sl=sl: v.tensor_tensor(
                            mus[:, :], gbuf[:, sl, 512:544], gbuf[:, sl, 544:576], op=OP.add))
                        t_mt = p.layer("tensor", lambda t: t.transpose(
                            ps[2][0:32, 0:128], mus[:, :], ident[:, :]), waits=(t_mu2,))
                        crt = cosR0 if tile0 else cosRg
                        t_mr = p.layer("vector", lambda v: v.tensor_copy(
                            mrot[0:32, 2, :], ps[2][0:32, 0:128]), waits=(t_mt,))
                        p.layer("vector", lambda v, crt=crt: v.tensor_tensor(
                            mrot[0:32, 0, :], mrot[0:32, 2, :], crt[0:32, 0, :], op=OP.mult))
                        t_mb = p.layer("vector", lambda v, crt=crt: v.tensor_tensor(
                            mrot[0:32, 1, :], mrot[0:32, 2, :], crt[0:32, 1, :], op=OP.mult))
                        t_mx = p.dmas("sync", lambda y, inc: [
                            inc(y.dma_start(out=mrot[0:16, 2, :], in_=mrot[16:32, 1, :])),
                            inc(y.dma_start(out=mrot[16:32, 2, :], in_=mrot[0:16, 1, :]))],
                            waits=(t_mb,), stream="sh")
                        t_mrot = p.layer("vector", lambda v: v.scalar_tensor_tensor(
                            mrot[0:32, 1, :], mrot[0:32, 2, :], sgn16[0:32, 0:1],
                            mrot[0:32, 0, :], op0=OP.mult, op1=OP.add), waits=(t_mx,))
                        t_rp = p.layer("tensor", lambda t: t.matmul(
                            ps[2][:, 0:128], rep4[0:32, :], mrot[0:32, 1, :],
                            start=True, stop=True, skip_group_check=True), waits=(t_mrot,))
                        p.layer("vector", lambda v, ckl=ckl: v.tensor_copy(
                            klrg[:, ckl * 128:(ckl + 1) * 128], ps[2][:, 0:128]), waits=(t_rp,))
                    def mms2(t, g=g):
                        o = None
                        for ch in range(2):
                            qh = qth[:, ch, g, :, :].rearrange("p a b -> p (a b)")
                            ql = qtl[:, ch, g, :, :].rearrange("p a b -> p (a b)")
                            o = t.matmul(ps[1][:, 0:512], qh, scdh[:, ch, :],
                                         start=(ch == 0), stop=False, skip_group_check=True)
                            o = t.matmul(ps[1][:, 0:512], qh, scdl[:, ch, :],
                                         start=False, stop=False, skip_group_check=True)
                            o = t.matmul(ps[1][:, 0:512], ql, scdh[:, ch, :],
                                         start=False, stop=False, skip_group_check=True)
                        for c in range(2):
                            lhs = lhS[:, c, g, :, :].rearrange("p a b -> p (a b)")
                            o = t.matmul(ps[1][:, 0:512], lhs, klrg[:, :],
                                         start=False, stop=(c == 1), skip_group_check=True)
                        return o

                    t_s2 = p.layer("tensor", mms2, waits=(p.last["vector"],))
                    if tile0:
                        t_d0 = p.dmas("sync", lambda y, inc, rd=rd: inc(y.dma_start(
                            out=dsel0[:, :, :],
                            in_=dsel0b_in.rearrange("p (d c q) -> p d c q", d=4, c=128)[:, rd, :, :])),
                            waits=(p.last["vector"],), stream="ld")
                        p.layer("vector", lambda v, rd=rd: v.tensor_tensor(
                            klrg[:, :].rearrange("p (c q) -> p c q", c=128),
                            ps[1][:, 0:512].rearrange("p (q c) -> p c q", q=4),
                            dsel0[:, :, :], op=OP.mult), waits=(t_s2, t_d0))
                        p.layer("vector", lambda v: v.tensor_reduce(
                            rotL[:, 0, :], klrg[:, :].rearrange("p (c q) -> p c q", c=128),
                            axis=AXX, op=OP.add))
                        if rd == 0:
                            p.layer("vector", lambda v: v.tensor_copy(S_sb[:, 0:128],
                                                                      rotL[:, 0, :]))
                        else:
                            p.layer("vector", lambda v: v.tensor_tensor(
                                S_sb[:, 0:128], S_sb[:, 0:128], rotL[:, 0, :], op=OP.add))
                    else:
                        p.layer("vector", lambda v: v.tensor_tensor(
                            klrg[:, :].rearrange("p (r q) -> p r q", r=32),
                            ps[1][:, 0:512].rearrange("p (q r) -> p r q", q=16),
                            dsel[:, :, :], op=OP.mult), waits=(t_s2,))
                        p.layer("vector", lambda v: v.tensor_reduce(
                            S_sb[:, 256:288], klrg[:, :].rearrange("p (r q) -> p r q", r=32),
                            axis=AXX, op=OP.add))
                if tile0:
                    p.layer("vector", lambda v: v.memset(S_sb[:, 96:128], -1.0e30))
                    SW = 128
                else:
                    u0b = (15 + i) * 128 if g < 4 else (16 + i) * 128
                    nbc = 256 if g < 4 else 128

                    def mmb(t, g=g, u0b=u0b, nbc=nbc):
                        o = None
                        for ch in range(2):
                            qh = qth[:, ch, g, :, :].rearrange("p a b -> p (a b)")
                            ql = qtl[:, ch, g, :, :].rearrange("p a b -> p (a b)")
                            o = t.matmul(ps[0][:, 0:nbc], qh, ckv_hi[:, ch, u0b:u0b + nbc],
                                         start=(ch == 0), stop=False, skip_group_check=True)
                            o = t.matmul(ps[0][:, 0:nbc], qh, ckv_lo[:, ch, u0b:u0b + nbc],
                                         start=False, stop=False, skip_group_check=True)
                            o = t.matmul(ps[0][:, 0:nbc], ql, ckv_hi[:, ch, u0b:u0b + nbc],
                                         start=False, stop=False, skip_group_check=True)
                        for c in range(2):
                            lhs = lhL[:, c, g, :, :].rearrange("p a b -> p (a b)")
                            o = t.matmul(ps[0][:, 0:nbc], lhs, klr4[:, u0b:u0b + nbc],
                                         start=False, stop=(c == 1), skip_group_check=True)
                        return o

                    t_sb = p.layer("tensor", mmb, waits=(p.last["vector"],))
                    p.layer("vector", lambda v, g=g: v.tensor_tensor(
                        S_sb[:, 0:256], ps[0][:, 0:256], bm01[:, g, :], op=OP.mult),
                        waits=(t_sb,))
                    p.layer("vector", lambda v, g=g: v.tensor_tensor(
                        S_sb[:, 0:256], S_sb[:, 0:256], bmng[:, g, :], op=OP.add))
                    SW = 288
                p.layer("vector", lambda v, SW=SW: v.tensor_reduce(
                    smx[:, 0:1], S_sb[:, 0:SW], axis=AXX, op=OP.max))
                p.layer("vector", lambda v: v.tensor_scalar(
                    smx[:, 1:2], smx[:, 0:1], -SC1, None, op0=OP.mult))
                t_ex = p.layer("scalar", lambda s, SW=SW: s.activation(
                    S_sb[:, 0:SW], S_sb[:, 0:SW], ACT.Exp, scale=SC1, bias=smx[:, 1:2]),
                    waits=(p.last["vector"],))
                p.layer("vector", lambda v, SW=SW: v.tensor_reduce(
                    smx[:, 2:3], S_sb[:, 0:SW], axis=AXX, op=OP.add), waits=(t_ex,))
                p.layer("vector", lambda v: v.reciprocal(smx[:, 3:4], smx[:, 2:3]))
                p.layer("vector", lambda v, SW=SW: v.tensor_scalar(
                    S_sb[:, 0:SW], S_sb[:, 0:SW], smx[:, 3:4], None, op0=OP.mult))
                p.layer("vector", lambda v, SW=SW: v.tensor_copy(Ph[:, 0:SW], S_sb[:, 0:SW]))
                t_pl = p.layer("vector", lambda v, SW=SW: v.tensor_tensor(
                    Pl[:, 0:SW], S_sb[:, 0:SW], Ph[:, 0:SW], op=OP.subtract))
                if tile0:
                    t_pt = p.layer("tensor", lambda t: t.transpose(
                        ps[2][:, 0:128], S_sb[:, 0:128], ident[:, :]), waits=(t_pl,))
                    p.layer("vector", lambda v: v.tensor_copy(PTsh[:, :], ps[2][:, 0:128]),
                            waits=(t_pt,))
                    p.layer("vector", lambda v: v.tensor_tensor(
                        PTsl[:, :], ps[2][:, 0:128], PTsh[:, :], op=OP.subtract))
                else:
                    def ptr(t):
                        t.transpose(ps[2][:, 0:128], S_sb[:, 0:128], ident[:, :])
                        t.transpose(ps[2][:, 128:256], S_sb[:, 128:256], ident[:, :])
                        return t.transpose(ps[2][0:32, 256:384], S_sb[:, 256:288], ident[:, :])

                    t_pt = p.layer("tensor", ptr, waits=(t_pl,))
                    p.layer("vector", lambda v: v.tensor_copy(
                        PTbh[:, :, :], ps[2][:, 0:256].rearrange("p (c w) -> p c w", c=2)),
                        waits=(t_pt,))
                    p.layer("vector", lambda v: v.tensor_tensor(
                        PTbl[:, :, :], ps[2][:, 0:256].rearrange("p (c w) -> p c w", c=2),
                        PTbh[:, :, :], op=OP.subtract))
                    p.layer("vector", lambda v: v.tensor_copy(mrot[0:32, 0, :],
                                                              ps[2][0:32, 256:384]))
                    p.layer("vector", lambda v: v.tensor_copy(PTsh[0:32, :], mrot[0:32, 0, :]))
                    p.layer("vector", lambda v: v.tensor_tensor(
                        PTsl[0:32, :], mrot[0:32, 0, :], PTsh[0:32, :], op=OP.subtract))
                if tile0:
                    t_vdma = None
                    for q in range(16):
                        sl = q

                        def mmv0(t, q=q, sl=sl):
                            o = t.matmul(ps[3][0:8, 0:256], PTsh[:, q * 8:(q + 1) * 8],
                                         gbuf[:, sl, 0:256], start=True, stop=False,
                                         skip_group_check=True)
                            o = t.matmul(ps[3][0:8, 0:256], PTsh[:, q * 8:(q + 1) * 8],
                                         gbuf[:, sl, 256:512], start=False, stop=False,
                                         skip_group_check=True)
                            return t.matmul(ps[3][0:8, 0:256], PTsl[:, q * 8:(q + 1) * 8],
                                            gbuf[:, sl, 0:256], start=False, stop=True,
                                            skip_group_check=True)

                        t_v0 = p.layer("tensor", mmv0, waits=(p.last["vector"],))
                        t_st8 = p.layer("vector", lambda v: v.tensor_copy(
                            stg8[0:8, :], ps[3][0:8, 0:256]), waits=(t_v0,))
                        t_vdma = p.dmas("sync", lambda y, inc, q=q: inc(
                            y.dma_start(out=AGG[q * 8:(q + 1) * 8, :], in_=stg8[0:8, :])),
                            waits=(t_st8,), stream="sh")
                    p.wait("tensor", t_vdma)
                    p.wait("vector", t_vdma)
                    t_agg = p.dma_tok("sh")
                else:
                    def mmv(t, g=g):
                        o = None
                        chs = ((0, 0), (1, 1)) if g < 4 else ((0, 1),)
                        first = True
                        for (pc, tc) in chs:
                            o = t.matmul(ps[3][:, 0:256], PTbh[:, pc, :], tmbh[:, tc, :],
                                         start=first, stop=False, skip_group_check=True)
                            first = False
                            o = t.matmul(ps[3][:, 0:256], PTbh[:, pc, :], tmbl[:, tc, :],
                                         start=False, stop=False, skip_group_check=True)
                            o = t.matmul(ps[3][:, 0:256], PTbl[:, pc, :], tmbh[:, tc, :],
                                         start=False, stop=False, skip_group_check=True)
                        return o

                    t_vb = p.layer("tensor", mmv, waits=(p.last["vector"],))
                    for ckl in range(4):
                        sl = gsl + ckl
                        t_rp2 = p.layer("tensor", lambda t: t.matmul(
                            ps[2][:, 0:128], rep4[0:32, :], mrot[0:32, 0, :],
                            start=True, stop=True, skip_group_check=True),
                            waits=(p.last["vector"],))
                        p.layer("vector", lambda v, ckl=ckl: v.tensor_tensor(
                            lkh[:, :], ps[2][:, 0:128], mck[:, ckl, :], op=OP.mult),
                            waits=(t_rp2,))
                        p.layer("vector", lambda v: v.tensor_tensor(
                            lkl[:, :], ps[2][:, 0:128], lkh[:, :], op=OP.subtract))
                        t_lk = p.layer("vector", lambda v, ckl=ckl: v.tensor_tensor(
                            lkl[:, :], lkl[:, :], mck[:, ckl, :], op=OP.mult))

                        def mmvs(t, sl=sl):
                            o = t.matmul(ps[3][:, 0:256], lkh[:, :], gbuf[:, sl, 0:256],
                                         start=False, stop=False, skip_group_check=True)
                            o = t.matmul(ps[3][:, 0:256], lkh[:, :], gbuf[:, sl, 256:512],
                                         start=False, stop=False, skip_group_check=True)
                            return t.matmul(ps[3][:, 0:256], lkl[:, :], gbuf[:, sl, 0:256],
                                            start=False, stop=(sl == gsl + 3),
                                            skip_group_check=True)

                        p.layer("tensor", mmvs, waits=(t_lk,))
                    t_agg = p.layer("vector", lambda v: v.tensor_copy(
                        AGG[:, :], ps[3][:, 0:256]), waits=(p.last["tensor"],))
                for ch in range(2):
                    t_at = p.layer("tensor", lambda t, ch=ch: t.transpose(
                        ps[2][:, 0:128], AGG[:, ch * 128:(ch + 1) * 128], ident[:, :]),
                        waits=(t_agg, p.last["vector"]))
                    p.layer("vector", lambda v, ch=ch, g=g: v.tensor_copy(
                        AGTh[:, ch, :, g * 16:(g + 1) * 16],
                        ps[2][:, 0:128].rearrange("p (q h) -> p h q", q=16)), waits=(t_at,))
                    p.layer("vector", lambda v, ch=ch, g=g: v.tensor_tensor(
                        AGTl[:, ch, :, g * 16:(g + 1) * 16],
                        ps[2][:, 0:128].rearrange("p (q h) -> p h q", q=16),
                        AGTh[:, ch, :, g * 16:(g + 1) * 16], op=OP.subtract))
        uix_glob[0] += nun
        for h in range(8):
            psb = ps[4] if h < 4 else ps[6]
            col = (h % 4) * 128

            def mm1(t, h=h, psb=psb, col=col):
                o = None
                for ch in range(2):
                    o = t.matmul(psb[0:64, col:col + 128], wuvh[:, ch, h * 64:(h + 1) * 64],
                                 AGTh[:, ch, h, :], start=(ch == 0), stop=False,
                                 skip_group_check=True)
                    o = t.matmul(psb[0:64, col:col + 128], wuvh[:, ch, h * 64:(h + 1) * 64],
                                 AGTl[:, ch, h, :], start=False, stop=False,
                                 skip_group_check=True)
                    o = t.matmul(psb[0:64, col:col + 128], wuvl[:, ch, h * 64:(h + 1) * 64],
                                 AGTh[:, ch, h, :], start=False, stop=(ch == 1),
                                 skip_group_check=True)
                return o

            t_h1 = p.layer("tensor", mm1, waits=(p.last["vector"],))
            p.layer("vector", lambda v, h=h, psb=psb, col=col: v.tensor_copy(
                hoth[0:64, h, :], psb[0:64, col:col + 128]), waits=(t_h1,))
            p.layer("vector", lambda v, h=h, psb=psb, col=col: v.tensor_tensor(
                hotl[0:64, h, :], psb[0:64, col:col + 128], hoth[0:64, h, :],
                op=OP.subtract))
        for h in range(8):
            t_wl = p.dmas("sync", lambda y, inc, h=h: inc(
                y.dma_start(out=ldb[0:64, :], in_=W["wout"][h * 64:(h + 1) * 64, :])),
                waits=(p.last["vector"], p.last["tensor"]), stream="w")
            p.layer("vector", lambda v: v.tensor_copy(woh[0:64, :], ldb[0:64, :]),
                    waits=(t_wl,))
            t_wsp = p.layer("vector", lambda v: v.tensor_tensor(
                wol[0:64, :], ldb[0:64, :], woh[0:64, :], op=OP.subtract))

            def mm2(t, h=h):
                o = None
                for nh in range(2):
                    psb = ps[0] if nh == 0 else ps[1]
                    o = t.matmul(psb[:, 0:512], hoth[0:64, h, :], woh[0:64, nh * 512:(nh + 1) * 512],
                                 start=(h == 0), stop=False, skip_group_check=True)
                    o = t.matmul(psb[:, 0:512], hoth[0:64, h, :], wol[0:64, nh * 512:(nh + 1) * 512],
                                 start=False, stop=False, skip_group_check=True)
                    o = t.matmul(psb[:, 0:512], hotl[0:64, h, :], woh[0:64, nh * 512:(nh + 1) * 512],
                                 start=False, stop=(h == 7), skip_group_check=True)
                return o

            p.layer("tensor", mm2, waits=(t_wsp,))
        p.layer("vector", lambda v: v.tensor_copy(s1sb[:, 0:512], ps[0][:, 0:512]),
                waits=(p.last["tensor"],))
        t_s1 = p.layer("vector", lambda v: v.tensor_copy(s1sb[:, 512:1024], ps[1][:, 0:512]))
        t_s1d = p.dmas("sync", lambda y, inc, i=i: inc(
            y.dma_start(out=s1_hbm[i * 128:(i + 1) * 128, :], in_=s1sb[:, :])),
            waits=(t_s1,), stream="sp")
        p.wait("vector", t_s1d)

    if stage == "l1p7":
        early_out()
        return
    newphase(m_tab)
    cnt8 = A.take((128, 8), F32)
    cpr = A.take((128, 1), F32)
    erow8 = A.take((128, 1), F32)
    e8968 = A.take((128, 1), F32)
    clt = A.take((128, 1), F32)
    pb8 = A.take((128, 1), F32)
    rhs3 = A.take((128, 3), F32)
    wt3 = A.take((128, 3), F32)
    wt1 = A.take((128, 4), F32)
    kp_t = A.take((128, NT), F32)
    pos_t = A.take((128, NT), F32)
    rid_t = A.take((128, NT), F32)
    cwr = A.take((128, 128), I16)
    wstg = A.take((128, 8, 28), F32)
    wrf2 = A.take((128, 224), F32)
    val64 = A.take((128, NT, 64), F32)
    zer64 = A.take((128, 64), F32)
    pidx = A.take((128, 128), I16)
    ordx = A.take((128, 3, 28), F32)
    ordld = A.take((128, 28, 64), F32)
    wdsp = A.take((128, 224), I16)
    b1sb = A.take((128, 128), F32)
    b2sb = A.take((128, 1024), F32)
    gx = A.take((128, 7, 1024), F32)
    xdh = A.take((128, 7, 1024), BF16)
    xdl = A.take((128, 7, 1024), BF16)
    xdTh = A.take((128, 8, 512), BF16)
    xdTl = A.take((128, 8, 512), BF16)
    w1s = A.take((128, 8, 128), F32)
    w1h = A.take((128, 8, 128), BF16)
    w1l = A.take((128, 8, 128), BF16)
    w2s = A.take((128, 1024), F32)
    w2h = A.take((128, 1024), BF16)
    w2l = A.take((128, 1024), BF16)
    hsf = A.take((128, 512), F32)
    hh_ = A.take((128, 32, 512), BF16)
    hl_ = A.take((128, 32, 512), BF16)
    eob = A.take((128, 1024), F32)
    x1g = A.take((128, 1024), F32)
    s1g = A.take((128, 1024), F32)

    pb8v = ps[2][:, :].bitcast(BF16)
    t_p8 = p.dmas("sync", lambda y, inc: [
        inc(y.dma_start(out=cnt8[0:8, :], in_=dram["cnt_out"].ap()[:, :])),
        inc(y.dma_start(out=cpr[0:8, 0:1],
                        in_=dram["cpr_out"].ap()[0:1, :].rearrange("a e -> (a e) ()"))),
        inc(y.dma_start(out=erow8[0:8, :], in_=cin["erow"][:, :])),
        inc(y.dma_start(out=e8968[0:8, :], in_=cin["e896"][:, :])),
        inc(y.dma_start(out=clt[0:8, :], in_=cin["corelt"][:, :])),
        inc(y.dma_start(out=b1sb[:, :], in_=W["b1g"][:, :]))], stream="ld")
    p.layer("vector", lambda v: v.tensor_scalar(pb8[0:8, :], cpr[0:8, :], sc1["jf"][0:8, 0:1],
                                                None, op0=OP.mult), waits=(t_p8,))
    t_gb = p.layer("tensor", lambda t: t.matmul(ps[5][0:8, 0:1], cnt8[0:8, :], clt[0:8, 0:1],
                                                start=True, stop=True, skip_group_check=True),
                   waits=(p.last["vector"],))
    p.layer("vector", lambda v: v.tensor_copy(rhs3[0:8, 0:1], ps[5][0:8, 0:1]), waits=(t_gb,))
    p.layer("vector", lambda v: v.tensor_tensor(rhs3[0:8, 1:2], erow8[0:8, :], pb8[0:8, :],
                                                op=OP.add))
    p.layer("vector", lambda v: v.tensor_tensor(rhs3[0:8, 2:3], e8968[0:8, :], pb8[0:8, :],
                                                op=OP.add))
    p.layer("vector", lambda v: v.memset(val64[:, :, :], 0.0))
    t_z = p.layer("vector", lambda v: v.memset(zer64[:, :], 0.0))
    t_zo = p.dmas("sync", lambda y, inc: [inc(y.dma_start(
        out=dram["ord_sh"].ap()[b * 128:(b + 1) * 128, :], in_=zer64[:, :]))
        for b in range(57)], waits=(t_z,), stream="sp")
    t_pw = None
    for i in range(NT):
        t_m3 = p.layer("tensor", lambda t, i=i: t.matmul(ps[5][:, 0:3], ohT_t[0:8, i, :],
                                                         rhs3[0:8, 0:3], start=True, stop=True,
                                                         skip_group_check=True),
                       waits=(p.last["vector"],))
        p.layer("vector", lambda v: v.tensor_copy(wt3[:, :], ps[5][:, 0:3]), waits=(t_m3,))
        p.layer("vector", lambda v, i=i: v.tensor_tensor(wt1[:, 0:1], wt3[:, 0:1],
                                                         rnk_t[:, i:i + 1], op=OP.add))
        p.layer("vector", lambda v: v.tensor_scalar(wt1[:, 1:2], wt1[:, 0:1], float(CAP), None,
                                                    op0=OP.is_lt))
        p.layer("vector", lambda v, i=i: v.tensor_tensor(kp_t[:, i:i + 1], wt1[:, 1:2],
                                                         psl_t[:, i:i + 1], op=OP.mult))
        p.layer("vector", lambda v, i=i: v.tensor_tensor(wt1[:, 2:3], wt3[:, 1:2],
                                                         rnk_t[:, i:i + 1], op=OP.add))
        p.layer("vector", lambda v, i=i: v.tensor_scalar(rid_t[:, i:i + 1], wt1[:, 2:3],
                                                         7167.0, None, op0=OP.min))
        p.layer("vector", lambda v, i=i: v.tensor_tensor(wt1[:, 3:4], wt3[:, 2:3],
                                                         rnk_t[:, i:i + 1], op=OP.add))
        p.layer("vector", lambda v, i=i: v.tensor_scalar(pos_t[:, i:i + 1], wt1[:, 3:4],
                                                         7168.0, None, op0=OP.min))
        p.layer("vector", lambda v, i=i: v.tensor_scalar(wt1[:, 0:1], piota[:, 0:1],
                                                         float(i * 128), None, op0=OP.add))
        p.layer("vector", lambda v, i=i: v.scalar_tensor_tensor(
            val64[:, i, 0:1], sc1["jf"][:, 0:1], 2048.0, wt1[:, 0:1], op0=OP.mult, op1=OP.add))
    def wrap16b(src_ap, ncol, wrout, waits=()):
        t_sh = p.dmas("sync", lambda y, inc: [inc(y.dma_start(
            out=wstg[0:16, ph, 0:ncol], in_=src_ap[ph * 16:(ph + 1) * 16, 0:ncol]))
            for ph in range(8)], waits=tuple(waits) + (p.last["vector"],), stream="sh")
        wrv = wrf2[:, 0:ncol * 8].rearrange("p (c h) -> p c h", h=8)
        p.layer("vector", lambda v: v.tensor_copy(
            wrv[0:16, :, :], wstg[0:16, :, 0:ncol].rearrange("p a b -> p b a")),
            waits=(t_sh,))
        t_cv = p.layer("vector", lambda v: v.tensor_copy(wrout[0:16, 0:ncol * 8],
                                                         wrf2[0:16, 0:ncol * 8]))
        t_rp = p.dmas("sync", lambda y, inc: [inc(y.dma_start(
            out=wrout[g * 16:(g + 1) * 16, 0:ncol * 8], in_=wrout[0:16, 0:ncol * 8]))
            for g in range(1, 8)], waits=(t_cv,), stream="sh")
        p.wait("vector", t_rp)
        return t_rp

    t_pi = wrap16b(pos_t, 16, pidx)
    t_cw = wrap16b(rid_t, 16, cwr)
    p.wait("gpsimd", t_zo, t_pi, t_cw, p.last["vector"], p.dma_tok("sp"))
    t_sc = p.dmas("gpsimd", lambda g, inc: inc(g.dma_scatter_add(
        dram["ord_sh"].ap(), val64[:, :, :], pidx[:, :], num_idxs=2048, num_idxs_reg=2048,
        elem_size=64, single_packet=False)), stream="gx")
    p.wait("gpsimd", t_sc)
    t_b2 = p.layer("gpsimd", lambda g: g.collective_compute(
        "AllReduce", mybir.AluOpType.add, replica_groups=[[0, 1], [2, 3], [4, 5], [6, 7]],
        ins=[dram["ord_sh"].ap().opt()], outs=[dram["ord_red"].ap().opt()]))
    t_oa = p.dmas("sync", lambda y, inc: inc(
        y.dma_start(out=ordld[:, :, :],
                    in_=dram["ord_red"].ap()[0:3584, :].rearrange("(b p) c -> p b c", p=128))),
        waits=(t_b2,), stream="ld")
    p.layer("vector", lambda v: v.tensor_copy(ordx[:, 0, :], ordld[:, :, 0]), waits=(t_oa,))
    t_ob = p.dmas("sync", lambda y, inc: inc(
        y.dma_start(out=ordld[:, :, :],
                    in_=dram["ord_red"].ap()[3584:7168, :].rearrange("(b p) c -> p b c", p=128))),
        waits=(p.last["vector"],), stream="ld")
    p.layer("vector", lambda v: v.tensor_copy(ordx[:, 1, :], ordld[:, :, 0]), waits=(t_ob,))
    p.layer("vector", lambda v: v.tensor_tensor(ordx[:, 2, :], ordx[:, 1, :], ordx[:, 0, :],
                                                op=OP.subtract))
    p.layer("vector", lambda v: v.tensor_scalar(ordx[:, 2, :], ordx[:, 2, :], sc1["jf"][:, 0:1],
                                                None, op0=OP.mult))
    p.layer("vector", lambda v: v.tensor_tensor(ordx[:, 2, :], ordx[:, 2, :], ordx[:, 0, :],
                                                op=OP.add))
    t_dl = wrap16b(ordx[:, 2, :], 28, wdsp)
    for e in range(4):
        t_b2l = p.dmas("sync", lambda y, inc, e=e: inc(y.dma_start(
            out=b2sb[0:1, :], in_=W["b2g"][e:e + 1, :])),
            waits=(p.last["tensor"],), stream="w")
        t_gd = p.dmas("gpsimd", lambda g, inc, e=e: inc(g.dma_gather(
            gx[:, :, :], dram["xnp"], wdsp[:, e * 56:(e + 1) * 56], num_idxs=NSB,
            num_idxs_reg=NSB, elem_size=1024, single_packet=False)),
            waits=(t_dl, p.last["vector"], p.last["tensor"]), stream="gx")
        p.wait("vector", t_gd)
        p.layer("vector", lambda v: v.tensor_copy(xdh[:, :, :], gx[:, :, :]))
        p.layer("vector", lambda v: v.tensor_tensor(
            xdl[:, :, :], gx[:, :, :], xdh[:, :, :], op=OP.subtract))
        for off, ns in ((0, 512), (512, 384)):
            nsl = ns // 128
            sb0 = off // 128
            for sl0 in range(nsl):
                sl = sb0 + sl0
                for ch in range(8):
                    def trx(t, sl=sl, ch=ch):
                        t.transpose(pb8v[:, 0:128], xdh[:, sl, ch * 128:(ch + 1) * 128],
                                    identb[:, :])
                        return t.transpose(pb8v[:, 128:256], xdl[:, sl, ch * 128:(ch + 1) * 128],
                                           identb[:, :])

                    t_tx = p.layer("tensor", trx, waits=(p.last["vector"],))
                    p.layer("vector", lambda v, sl0=sl0, ch=ch: v.tensor_copy(
                        xdTh[:, ch, sl0 * 128:(sl0 + 1) * 128], pb8v[:, 0:128]), waits=(t_tx,))
                    p.layer("vector", lambda v, sl0=sl0, ch=ch: v.tensor_copy(
                        xdTl[:, ch, sl0 * 128:(sl0 + 1) * 128], pb8v[:, 128:256]))
            for m in range(32):
                t_w1 = p.dmas("sync", lambda y, inc, e=e, m=m: inc(y.dma_start(
                    out=w1s[:, :, :],
                    in_=W["w1g"][e * 1024:(e + 1) * 1024, m * 128:(m + 1) * 128]
                    .rearrange("(ch p) c -> p ch c", p=128))),
                    waits=(p.last["vector"],), stream="w")
                p.layer("vector", lambda v: v.tensor_copy(w1h[:, :, :], w1s[:, :, :]),
                        waits=(t_w1,))
                t_w1s = p.layer("vector", lambda v: v.tensor_tensor(
                    w1l[:, :, :], w1s[:, :, :], w1h[:, :, :], op=OP.subtract))

                def mm1(t, ns=ns):
                    o = None
                    for ch in range(8):
                        o = t.matmul(ps[0][:, 0:ns], w1h[:, ch, :], xdTh[:, ch, 0:ns],
                                     start=(ch == 0), stop=False, skip_group_check=True)
                        o = t.matmul(ps[0][:, 0:ns], w1h[:, ch, :], xdTl[:, ch, 0:ns],
                                     start=False, stop=False, skip_group_check=True)
                        o = t.matmul(ps[0][:, 0:ns], w1l[:, ch, :], xdTh[:, ch, 0:ns],
                                     start=False, stop=(ch == 7), skip_group_check=True)
                    return o

                t_m1 = p.layer("tensor", mm1, waits=(t_w1s,))
                t_rl = p.layer("scalar", lambda s, e=e, m=m, ns=ns: s.activation(
                    hsf[:, 0:ns], ps[0][:, 0:ns], ACT.Relu, bias=b1sb[:, e * 32 + m:e * 32 + m + 1]),
                    waits=(t_m1, p.last["vector"]))
                p.layer("vector", lambda v, m=m, ns=ns: v.tensor_copy(hh_[:, m, 0:ns],
                                                                      hsf[:, 0:ns]),
                        waits=(t_rl,))
                p.layer("vector", lambda v, m=m, ns=ns: v.tensor_tensor(
                    hl_[:, m, 0:ns], hsf[:, 0:ns], hh_[:, m, 0:ns], op=OP.subtract))
            for m in range(32):
                t_w2 = p.dmas("sync", lambda y, inc, e=e, m=m: inc(y.dma_start(
                    out=w2s[:, :], in_=W["w2g"][e * DFF + m * 128:e * DFF + (m + 1) * 128, :])),
                    waits=(p.last["tensor"],), stream="w")
                p.layer("vector", lambda v: v.tensor_copy(w2h[:, :], w2s[:, :]), waits=(t_w2,))
                t_w2s = p.layer("vector", lambda v: v.tensor_tensor(
                    w2l[:, :], w2s[:, :], w2h[:, :], op=OP.subtract))

                def mm2(t, m=m, nsl=nsl):
                    o = None
                    for sb in range(nsl):
                        for nh in range(2):
                            psb = ps[2 * sb + nh]
                            o = t.matmul(psb[:, 0:512], hh_[:, m, sb * 128:(sb + 1) * 128],
                                         w2h[:, nh * 512:(nh + 1) * 512], start=(m == 0),
                                         stop=False, skip_group_check=True)
                            o = t.matmul(psb[:, 0:512], hh_[:, m, sb * 128:(sb + 1) * 128],
                                         w2l[:, nh * 512:(nh + 1) * 512], start=False,
                                         stop=False, skip_group_check=True)
                            o = t.matmul(psb[:, 0:512], hl_[:, m, sb * 128:(sb + 1) * 128],
                                         w2h[:, nh * 512:(nh + 1) * 512], start=False,
                                         stop=False, skip_group_check=True)
                    return o

                p.layer("tensor", mm2, waits=(t_w2s,))
            p.wait("tensor", t_b2l)

            def mmb2(t, nsl=nsl):
                o = None
                for sb in range(nsl):
                    for nh in range(2):
                        psb = ps[2 * sb + nh]
                        o = t.matmul(psb[:, 0:512], ones_row128[0:1, 0:128],
                                     b2sb[0:1, nh * 512:(nh + 1) * 512], start=False,
                                     stop=True, skip_group_check=True)
                return o

            t_bb = p.layer("tensor", mmb2)
            for sb in range(nsl):
                p.layer("vector", lambda v, sb=sb: v.tensor_copy(eob[:, 0:512],
                                                                 ps[2 * sb][:, 0:512]),
                        waits=(t_bb,) if sb == 0 else ())
                t_eo = p.layer("vector", lambda v, sb=sb: v.tensor_copy(
                    eob[:, 512:1024], ps[2 * sb + 1][:, 0:512]))
                t_ed = p.dmas("sync", lambda y, inc, e=e, off=off, sb=sb: inc(y.dma_start(
                    out=dram["eout_loc"].ap()[e * NSB + off + sb * 128:
                                              e * NSB + off + (sb + 1) * 128, :],
                    in_=eob[:, :])), waits=(t_eo,), stream="sp")
                p.wait("vector", t_ed)
    p.wait("gpsimd", p.dma_tok("sp"), p.last["vector"])
    t_ex = p.layer("gpsimd", lambda g: g.collective_compute(
        "AllGather", mybir.AluOpType.bypass, replica_groups=[[0, 1], [2, 3], [4, 5], [6, 7]],
        ins=[dram["eout_loc"].ap().opt()], outs=[dram["eout_pair"].ap().opt()]))

    t_out = None
    for ib in range(4):
        t_cg = p.dmas("gpsimd", lambda g, inc, ib=ib: inc(g.dma_gather(
            gx[:, 0:4, :], dram["eout_pair"].ap(), cwr[:, ib * 32:(ib + 1) * 32],
            num_idxs=512, num_idxs_reg=512, elem_size=1024, single_packet=False)),
            waits=(p.last["vector"], p.last["scalar"]), stream="gx")
        p.wait("vector", t_cg)
        p.wait("scalar", t_cg)
        for t_ in range(4):
            i = ib * 4 + t_
            t_x1l = p.dmas("sync", lambda y, inc, i=i: [
                inc(y.dma_start(out=x1g[:, :], in_=x1_hbm[i * 128:(i + 1) * 128, :])),
                inc(y.dma_start(out=s1g[:, :], in_=s1_hbm[i * 128:(i + 1) * 128, :]))],
                waits=(p.last["vector"],), stream="ld")
            p.layer("vector", lambda v, i=i, t_=t_: v.tensor_scalar(
                gx[:, t_, :], gx[:, t_, :], kp_t[:, i:i + 1], None, op0=OP.mult))
            t_sil = p.layer("scalar", lambda s, t_=t_: s.activation(
                gx[:, t_, :], gx[:, t_, :], ACT.Silu), waits=(p.last["vector"],))
            p.layer("vector", lambda v, t_=t_: v.tensor_tensor(
                s1g[:, :], s1g[:, :], gx[:, t_, :], op=OP.mult), waits=(t_sil, t_x1l))
            p.layer("vector", lambda v: v.tensor_tensor(x1g[:, :], x1g[:, :], s1g[:, :],
                                                        op=OP.add))
            t_out = p.dmas("sync", lambda y, inc, i=i: inc(y.dma_start(
                out=out[i * 128:(i + 1) * 128, :], in_=x1g[:, :])),
                waits=(p.last["vector"],), stream="sp")
            p.wait("vector", t_out)
    p.wait("sync", t_out)
    p.wait("tensor", t_out)
    return


def kernel(**inputs):
    x_full = np.ascontiguousarray(inputs["L0_attn_x"], dtype=np.float32)
    gw = np.ascontiguousarray(inputs["L0_ffn_gate_w"], dtype=np.float32)
    uw = np.ascontiguousarray(inputs["L0_ffn_up_w"], dtype=np.float32)
    dwn = np.ascontiguousarray(inputs["L0_ffn_down_w"], dtype=np.float32)
    gsx = np.ascontiguousarray(np.broadcast_to(
        np.asarray(inputs["L0_ffn_s_in"], np.float32).reshape(1, 1), (128, 1)))
    gsh = np.ascontiguousarray(np.broadcast_to(
        np.asarray(inputs["L0_ffn_s_in_down"], np.float32).reshape(1, 1), (128, 1)))
    wqT = np.ascontiguousarray(np.asarray(inputs["L0_attn_W_q"], np.float32).T)
    wkT = np.ascontiguousarray(np.asarray(inputs["L0_attn_W_k"], np.float32).T)
    wvT = np.ascontiguousarray(np.asarray(inputs["L0_attn_W_v"], np.float32).T)
    woT = np.ascontiguousarray(np.asarray(inputs["L0_attn_W_o"], np.float32).T)
    beta = np.ascontiguousarray(np.broadcast_to(
        np.asarray(inputs["L0_attn_beta_param"], np.float32).reshape(1, H0), (128, H0)))
    f32 = lambda k: np.ascontiguousarray(np.asarray(inputs[k], np.float32))
    W_uk = f32("L1_attn_W_uk")
    ukz = np.zeros((128, 8, 256), np.float32)
    for h in range(8):
        r0 = 64 * (h % 2)
        ukz[r0:r0 + 64, h, :] = W_uk[:, 64 * h:64 * (h + 1)].T
    l1c = {
        "l1w_dkv": f32("L1_attn_W_dkv"), "l1w_uv": f32("L1_attn_W_uv"),
        "l1w_dq": f32("L1_attn_W_dq"), "l1w_uq": f32("L1_attn_W_uq"),
        "l1w_qr": f32("L1_attn_W_qr"), "l1w_kr": f32("L1_attn_W_kr"),
        "l1w_wout": f32("L1_attn_W_out"), "l1w_iwq": f32("L1_attn_idx_wq"),
        "l1w_iwk": f32("L1_attn_idx_wk"),
        "l1w_gwT": np.ascontiguousarray(f32("L1_ffn_gate_w").T),
        "l1w_ukz": np.ascontiguousarray(ukz.reshape(128, 2048)),
        "l1w_idxw": np.ascontiguousarray(np.broadcast_to(
            np.asarray(inputs["L1_attn_idx_w"], np.float32).reshape(1, 4), (128, 4))),
        "identb_in": np.eye(128, dtype=ml_dtypes.bfloat16),
    }
    hc0 = host_consts(0)
    l1c["bm01b"] = hc0["bm01g"].astype(ml_dtypes.bfloat16)
    l1c["bmngb"] = hc0["bmngg"].astype(ml_dtypes.bfloat16)
    l1c["mckb"] = hc0["mck01"].astype(ml_dtypes.bfloat16)
    pp_ = np.arange(128)[:, None] // 8
    dsel = (np.arange(16)[None, :] == pp_).astype(ml_dtypes.bfloat16)
    l1c["dselb"] = np.ascontiguousarray(np.broadcast_to(
        dsel[:, None, :], (128, 32, 16)).reshape(128, 512))
    d0 = np.zeros((128, 4, 128, 4), np.float32)
    for rd in range(4):
        for qq in range(4):
            d0[(pp_[:, 0] == rd * 4 + qq), rd, :, qq] = 1.0
    l1c["dsel0b"] = np.ascontiguousarray(d0.reshape(128, 2048).astype(ml_dtypes.bfloat16))
    w1 = f32("L1_ffn_w1")
    w2 = f32("L1_ffn_w2")
    b1 = f32("L1_ffn_b1")
    b2 = f32("L1_ffn_b2")
    w1gj = [np.ascontiguousarray(w1[4 * j:4 * j + 4].reshape(4096, DFF)) for j in range(2)]
    w2gj = [np.ascontiguousarray(w2[4 * j:4 * j + 4].reshape(4 * DFF, D)) for j in range(2)]
    b2gj = [np.ascontiguousarray(b2[4 * j:4 * j + 4]) for j in range(2)]
    b1gj = []
    for j in range(2):
        bg = np.zeros((128, 128), np.float32)
        for e in range(4):
            for m in range(32):
                bg[:, e * 32 + m] = b1[4 * j + e, m * 128:(m + 1) * 128]
        b1gj.append(bg)
    nc = build_program(stage=STAGE)
    lower_extended_insts(nc)
    in_maps = []
    for c in range(8):
        b, j = c // 2, c % 2
        m = {"xa": np.ascontiguousarray(x_full[b, j * TH:(j + 1) * TH]),
             "xp": np.ascontiguousarray(x_full[b, 0:TH]),
             "gw": gw, "uw": uw, "dwn": dwn, "gsx_in": gsx, "gsh_in": gsh,
             "wqT": wqT, "wkT": wkT, "wvT": wvT, "woT": woT, "beta_in": beta}
        m.update(l1c)
        m.update({"l1w_w1g": w1gj[j], "l1w_w2g": w2gj[j], "l1w_b1g": b1gj[j],
                  "l1w_b2g": b2gj[j]})
        m.update(host_consts(c))
        in_maps.append(m)
    res = run_bass_kernel_spmd(nc, in_maps, core_ids=list(range(8)))
    outp = np.empty((B, T, D), np.float32)
    for c in range(8):
        b, j = c // 2, c % 2
        outp[b, j * TH:(j + 1) * TH] = res.results[c]["out"]
    return outp
